# version 06
# baseline (speedup 1.0000x reference)
import numpy as np
import ml_dtypes
import concourse.bass as bass
import concourse.mybir as mybir
from concourse.bass_utils import run_bass_kernel_spmd
from contextlib import ExitStack

F32 = mybir.dt.float32
BF16 = mybir.dt.bfloat16
AF = mybir.ActivationFunctionType
ALU = mybir.AluOpType
AX = mybir.AxisListType
BF = ml_dtypes.bfloat16

T, C, NH, R, NE, DI = 4096, 1024, 16, 64, 8, 512
QG = 512
NQG = T // QG
NCC = C // 128
NUC = T // 128
EGROUPS = [(0, 1, 2), (3, 4, 5), (6, 7, 8)]
NSLOT = 3
WCOLS = 4096
VSTRIDE = NH * 65

STAGE = 4
DBG_GROUPS = None
DBG_NO_CONV = False
DBG_NO_PREP = False
DBG_PREP_LEVEL = 4
DBG_NO_MEMSET = False
DBG_NQG = NQG
DBG_DMA_CHAIN = False
DBG_NO_FFN = False
DBG_NO_ROUTE = False
DBG_ROUTE_SYNC = True
DBG_SELF_SYNC = False
DBG_PE_FENCE = False
DBG_XFENCE = False
DBG_WARMUP = False
DBG_DRAIN = True

CONSTS = [1.0 / C, 1e-5, 1e-20, 1.25, 2.0, 4.0, 0.5, 0.03125, 1.0 / NH]
ENGS = ("sync", "gpsimd", "tensor", "scalar", "vector")


class Sem:
    __slots__ = ("h", "n", "owner", "dma")

    def __init__(self, h, owner, dma=False):
        self.h, self.n, self.owner, self.dma = h, 0, owner, dma

    def inc(self, k=1):
        self.n += k
        return self.n


class Prog:
    def __init__(self):
        self.th = {e: [] for e in ENGS}
        self.waited = {e: {} for e in ENGS}

    def emit(self, eng, op, waits=(), inc=None):
        w = []
        for it in waits:
            if it is None:
                continue
            s, v = it[0], it[1]
            if s is None or v is None or v <= 0:
                continue
            force = len(it) == 3 and it[2]
            if s.owner == eng and not s.dma and not force:
                continue
            if self.waited[eng].get(id(s), -1) >= v:
                continue
            self.waited[eng][id(s)] = v
            w.append((s, v))

        def f(e, op=op, w=w, inc=inc):
            for s, v in w:
                e.wait_ge(s.h, v)
            ins = op(e)
            if inc is not None:
                ins.then_inc(inc[0].h, inc[1])
        self.th[eng].append(f)


def build_program():
    nc = bass.Bass("TRN2", target_bir_lowering=False, detect_race_conditions=False)

    x_in = nc.dram_tensor("x_in", [T, C], F32, kind="ExternalInput").ap()
    basis_wt = nc.dram_tensor("basis_wt", [C, R], F32, kind="ExternalInput").ap()
    core_w = nc.dram_tensor("core_w", [R, R], F32, kind="ExternalInput").ap()
    head_res = nc.dram_tensor("head_res", [NH, R, R], F32, kind="ExternalInput").ap()
    v_wt = nc.dram_tensor("v_wt", [C, C], F32, kind="ExternalInput").ap()
    o_wt = nc.dram_tensor("o_wt", [C, C], F32, kind="ExternalInput").ap()
    gate_wt = nc.dram_tensor("gate_wt", [C, NE], F32, kind="ExternalInput").ap()
    corr_bc_in = nc.dram_tensor("corr_bc", [128, 32], F32, kind="ExternalInput").ap()
    vb_bc_in = nc.dram_tensor("vb_bc", [128, C], F32, kind="ExternalInput").ap()
    ob_pc_in = nc.dram_tensor("ob_pc", [128, NCC], F32, kind="ExternalInput").ap()
    Wg = nc.dram_tensor("Wg", [NE, C, DI], F32, kind="ExternalInput").ap()
    Wu = nc.dram_tensor("Wu", [NE, C, DI], F32, kind="ExternalInput").ap()
    Wdm = nc.dram_tensor("Wdm", [NE, DI, C], F32, kind="ExternalInput").ap()
    Wg_s = nc.dram_tensor("Wg_s", [C, DI], F32, kind="ExternalInput").ap()
    Wu_s = nc.dram_tensor("Wu_s", [C, DI], F32, kind="ExternalInput").ap()
    Wd_s = nc.dram_tensor("Wd_s", [DI, C], F32, kind="ExternalInput").ap()
    idf_in = nc.dram_tensor("idf", [128, 128], F32, kind="ExternalInput").ap()
    idb_in = nc.dram_tensor("idb", [128, 128], BF16, kind="ExternalInput").ap()
    ones_c = nc.dram_tensor("ones_c", [128, 1], BF16, kind="ExternalInput").ap()
    ones_r = nc.dram_tensor("ones_r", [65, 128], F32, kind="ExternalInput").ap()
    masks_in = nc.dram_tensor("masks", [128, 896], BF16, kind="ExternalInput").ap()
    consts_in = nc.dram_tensor("consts", [128, len(CONSTS)], F32, kind="ExternalInput").ap()
    out_hbm = nc.dram_tensor("out", [T, C], F32, kind="ExternalOutput").ap()
    x1s = nc.dram_tensor("x1s", [128, NCC * T], F32, kind="Internal").ap()
    wbf = nc.dram_tensor("wbf", [27, 128, WCOLS], BF16, kind="Internal").ap()

    prog = Prog()

    with ExitStack() as ctx:
        En = ctx.enter_context
        v_sb = En(nc.sbuf_tensor("v_sb", [128, NUC * VSTRIDE], BF16))
        latT2 = En(nc.sbuf_tensor("latT2", [128, T], BF16))
        wT_sb = En(nc.sbuf_tensor("wT_sb", [128, NCC * C], BF16))
        basis_wT = En(nc.sbuf_tensor("basis_wT", [128, NCC * R], BF16))
        gate_wT = En(nc.sbuf_tensor("gate_wT", [128, NCC * NE], BF16))
        hm2 = En(nc.sbuf_tensor("hm2", [128, NH * R], BF16))
        scr_f32 = En(nc.sbuf_tensor("scr_f32", [128, C], F32))
        vb_bc = scr_f32
        ob_col = En(nc.sbuf_tensor("ob_col", [128, NCC], F32))
        corr32 = En(nc.sbuf_tensor("corr32", [128, 32], F32))
        idf = En(nc.sbuf_tensor("idf_sb", [128, 128], F32))
        idb = En(nc.sbuf_tensor("idb_sb", [128, 128], BF16))
        onec = En(nc.sbuf_tensor("onec", [128, 1], BF16))
        oner = En(nc.sbuf_tensor("oner", [65, 128], F32))
        mask_sb = En(nc.sbuf_tensor("mask_sb", [128, 896], BF16))
        c_sb = En(nc.sbuf_tensor("c_sb", [128, len(CONSTS)], F32))
        stats = En(nc.sbuf_tensor("stats", [1, 1536], F32))
        fence_sb = En(nc.sbuf_tensor("fence_sb", [1, 8], F32))
        rt = En(nc.sbuf_tensor("rt", [128, 352], F32))
        cw_sb = En(nc.sbuf_tensor("cw_sb", [128, 32], F32))
        ybst = En(nc.sbuf_tensor("ybst", [64, 1024], BF16))
        stage = En(nc.sbuf_tensor("stage", [128, 2 * C], F32))
        x_sb = En(nc.sbuf_tensor("x_sb", [128, NCC * QG + 384], F32))
        xn_sb = En(nc.sbuf_tensor("xn_sb", [128, NCC * QG], BF16))
        h_sb = En(nc.sbuf_tensor("h_sb", [128, 3 * 4 * QG + 384], BF16))
        w_sb = En(nc.sbuf_tensor("w_sb", [128, NSLOT * WCOLS], BF16))
        sil_sb = En(nc.sbuf_tensor("sil_sb", [128, 2 * QG], F32))
        sq_sb = En(nc.sbuf_tensor("sq_sb", [128, 4 * QG], BF16))
        bc_sb = En(nc.sbuf_tensor("bc_sb", [128, 2 * QG], F32))
        htm_sb = En(nc.sbuf_tensor("htm_sb", [128, 2 * QG], BF16))
        exp_sb = En(nc.sbuf_tensor("exp_sb", [128, 2 * 1024], BF16))
        yT_sb = En(nc.sbuf_tensor("yT_sb", [128, NCC * QG], BF16))
        ltT_sb = En(nc.sbuf_tensor("ltT_sb", [128, 2 * QG], BF16))
        ps_a = En(nc.psum_tensor([128, 1024], F32))
        ps_b = En(nc.psum_tensor([128, 1024], F32))
        ps_c = En(nc.psum_tensor([128, 512], F32))
        ps_d = En(nc.psum_tensor([128, 512], F32))
        ps_e = En(nc.psum_tensor([128, 512], F32))
        ps_t = En(nc.psum_tensor([128, 1024], BF16))

        for ci, cv in enumerate(CONSTS):
            nc.const_aps.register(F32, cv, c_sb[:, ci:ci + 1])
        C05 = c_sb[:, CONSTS.index(0.5):CONSTS.index(0.5) + 1]

        sd = Sem(En(nc.semaphore("sd")), "sync", dma=True)
        gd = Sem(En(nc.semaphore("gd")), "gpsimd", dma=True)
        pe = Sem(En(nc.semaphore("pe")), "tensor")
        ac = Sem(En(nc.semaphore("ac")), "scalar")
        ve = Sem(En(nc.semaphore("ve")), "vector")
        blk = En(nc.Block())

        SELFSEM = {"scalar": ac, "vector": ve}

        def E(eng, op, waits=(), inc=None, k=1, drain=True):
            explicit = inc is not None
            if DBG_SELF_SYNC and eng in SELFSEM:
                sem = SELFSEM[eng]
                waits = list(waits) + [(sem, sem.n, True)]
                if inc is None:
                    inc = sem
            if DBG_DRAIN and explicit and drain and eng in ("tensor", "scalar", "vector"):
                prog.emit(eng, op, waits, None)
                tgt = inc.inc(k)

                def dop(e):
                    return e.drain()
                prog.emit(eng, dop, (), (inc, k))
                return tgt
            tgt = None
            if inc is not None:
                tgt = inc.inc(k)
            prog.emit(eng, op, waits, (inc, k) if inc is not None else None)
            return tgt

        def pe_drain_inc():
            def dop(e):
                return e.drain()
            tgt = pe.inc(1)
            prog.emit(eng="tensor", op=dop, waits=(), inc=(pe, 1))
            return tgt

        def dma(eng_sem, dst, src, waits=()):
            eng = "sync" if eng_sem is sd else "gpsimd"
            if DBG_DMA_CHAIN and eng_sem.n > 0:
                waits = list(waits) + [(eng_sem, eng_sem.n)]

            def op(e, dst=dst, src=src):
                return e.dma_start(dst, src)
            return E(eng, op, waits, eng_sem, 16)

        def pe_fence():
            def op(t):
                return t.transpose(ps_t[0:1, 768:769], onec[0:1, 0:1], idb[0:1, 0:1])
            return E("tensor", op, (), pe)

        def mm(out, lhsT, rhs, start, stop, tp=None, waits=(), inc=None, drain=True):
            def op(t, out=out, lhsT=lhsT, rhs=rhs, start=start, stop=stop, tp=tp):
                return t.matmul(out, lhsT, rhs, start=start, stop=stop, tile_position=tp)
            if (DBG_PE_FENCE or DBG_XFENCE) and not DBG_DRAIN and inc is pe:
                E("tensor", op, waits, None)
                return pe_fence()
            return E("tensor", op, waits, inc, drain=drain)

        def trans(out, in_, ident, waits=(), inc=None):
            def op(t, out=out, in_=in_, ident=ident):
                return t.transpose(out, in_, ident)
            if (DBG_PE_FENCE or DBG_XFENCE) and not DBG_DRAIN and inc is pe:
                E("tensor", op, waits, None)
                return pe_fence()
            return E("tensor", op, waits, inc)

        def act(out, in_, func, bias=0.0, scale=1.0, waits=(), inc=None):
            def op(s, out=out, in_=in_, func=func, bias=bias, scale=scale):
                return s.activation(out, in_, func, bias=bias, scale=scale)
            if DBG_XFENCE and not DBG_DRAIN and inc is not None:
                E("scalar", op, waits, None)

                def fop(s, func=func):
                    return s.activation(fence_sb[0:1, 2:3], c_sb[0:1, 0:1], func)
                return E("scalar", fop, (), inc)
            return E("scalar", op, waits, inc)

        def vop(name, *args, waits=(), inc=None):
            def op(v, name=name, args=args):
                return getattr(v, name)(*args)
            if DBG_XFENCE and not DBG_DRAIN and inc is not None:
                E("vector", op, waits, None)

                def fop(v):
                    return v.tensor_copy(fence_sb[0:1, 0:1], fence_sb[0:1, 1:2])
                return E("vector", fop, (), inc)
            return E("vector", op, waits, inc)

        bank = {}

        def brd(key, tgt):
            bank.setdefault(key, {})[id(tgt[0])] = tgt

        def bwt(key):
            return list(bank.get(key, {}).values())

        dma(sd, idf[:], idf_in)
        dma(sd, idb[:], idb_in)
        dma(sd, onec[:], ones_c)
        dma(sd, oner[:], ones_r)
        dma(sd, mask_sb[:], masks_in)
        t_c = dma(sd, c_sb[:], consts_in)
        t_bwt = dma(gd, basis_wT[:].rearrange("p (cc r) -> p cc r", cc=NCC),
                    basis_wt.rearrange("(cc p) r -> p cc r", p=128))
        t_gwt = dma(gd, gate_wT[:].rearrange("p (cc e) -> p cc e", cc=NCC),
                    gate_wt.rearrange("(cc p) e -> p cc e", p=128))
        t_corr = dma(sd, corr32[:], corr_bc_in)
        t_vbb = dma(sd, vb_bc[:, 0:C], vb_bc_in)
        t_ob = dma(sd, ob_col[:], ob_pc_in)
        t_hr = dma(sd, x_sb[0:64, 0:NH * R], head_res.rearrange("h r s -> r h s"))
        t_core = dma(sd, x_sb[0:64, 1024:1024 + R], core_w)
        vop("memset", fence_sb[0:1, 0:8], 1.0)
        for uc in range(NUC if not DBG_NO_MEMSET else 0):
            vop("memset",
                v_sb[:, uc * VSTRIDE:(uc + 1) * VSTRIDE]
                .rearrange("p (h d) -> p h d", h=NH)[:, :, 64:65], 1.0)

        corr_done = (sd, t_corr)
        vb_done = (sd, t_vbb)
        ob_done = (sd, t_ob)
        tpc = trans(ps_e[0:64, 0:64], x_sb[0:64, 1024:1024 + R], idf[0:64, 0:64],
                    waits=[(sd, t_core)], inc=pe)
        ce = vop("tensor_copy", x_sb[0:64, 1088:1088 + R], ps_e[0:64, 0:64],
                 waits=[(pe, tpc)], inc=ve)
        brd("e", (ve, ce))
        vop("tensor_add", x_sb[0:64, 1152:1216], x_sb[0:64, 1024:1088], x_sb[0:64, 1088:1152])
        vop("tensor_scalar", x_sb[0:64, 1152:1216], x_sb[0:64, 1152:1216], 0.03125, None, ALU.mult,
            waits=[(sd, t_c)])
        vop("tensor_add", x_sb[0:64, 1280:1792], x_sb[0:64, 0:512], x_sb[0:64, 512:1024],
            waits=[(sd, t_hr)])
        vop("tensor_add", x_sb[0:64, 1792:2048], x_sb[0:64, 1280:1536], x_sb[0:64, 1536:1792])
        vop("tensor_add", x_sb[0:64, 2048:2176], x_sb[0:64, 1792:1920], x_sb[0:64, 1920:2048])
        vop("tensor_add", x_sb[0:64, 1216:1280], x_sb[0:64, 2048:2112], x_sb[0:64, 2112:2176])
        vop("tensor_scalar", x_sb[0:64, 1216:1280], x_sb[0:64, 1216:1280], 1.0 / NH, None, ALU.mult)
        vop("tensor_sub", x_sb[0:64, 1216:1280], x_sb[0:64, 1216:1280], x_sb[0:64, 1152:1216])
        for h in range(NH):
            vop("tensor_sub", x_sb[0:64, 2176 + h * R:2176 + (h + 1) * R],
                x_sb[0:64, h * R:(h + 1) * R], x_sb[0:64, 1216:1280])
        hm_lo = vop("tensor_copy", hm2[0:64, :], x_sb[0:64, 2176:2176 + NH * R], inc=ve)
        hm_hi = dma(sd, hm2[64:128, :], hm2[0:64, :], waits=[(ve, hm_lo)])
        hm_done = (sd, hm_hi)

        vwT_last = None
        for cc in range(NCC):
            vwT_last = dma(gd, wT_sb[:, cc * C:(cc + 1) * C],
                           v_wt[cc * 128:(cc + 1) * 128, :])
        vwT_ev = (gd, vwT_last)

        morder = []
        for grp in EGROUPS:
            for e in grp:
                morder.append(("g", e))
                morder.append(("u", e))
            for e in grp:
                morder.append(("d", e))
        midx = {m: i for i, m in enumerate(morder)}

        def wsrc(kind, e):
            if kind == "g":
                ap = Wg[e] if e < NE else Wg_s
                return ap.rearrange("(k p) i -> p k i", p=128)
            if kind == "u":
                ap = Wu[e] if e < NE else Wu_s
                return ap.rearrange("(k p) i -> p k i", p=128)
            ap = Wdm[e] if e < NE else Wd_s
            return ap.rearrange("(k p) c -> p k c", p=128)

        slot_free = [None] * NSLOT

        def xck(cc):
            return x_sb[:, cc * QG:(cc + 1) * QG]

        def xnck(cc):
            return xn_sb[:, cc * QG:(cc + 1) * QG]

        def emit_ln(x_ready):
            sq_free = [None] * 4
            mt = None
            x_reads = []
            for cc in range(NCC):
                par = cc % 2
                cb = vop("tensor_copy", sq_sb[:, par * QG:(par + 1) * QG], xck(cc),
                         waits=[x_ready, sq_free[par]], inc=ve)
                sqt = act(sq_sb[:, (2 + par) * QG:(3 + par) * QG], xck(cc), AF.Square,
                          waits=[x_ready, sq_free[2 + par]], inc=ac)
                x_reads = [(ve, cb), (ac, sqt)]
                w0 = [(ve, cb)] + (bwt("e") if cc == 0 else [])
                mm(ps_e[0:1, 0:QG], onec[:], sq_sb[:, par * QG:(par + 1) * QG],
                   cc == 0, cc == NCC - 1, waits=w0)
                w1 = [(ac, sqt)] + (bwt("c") if cc == 0 else [])
                mt = mm(ps_c[0:1, 0:QG], onec[:], sq_sb[:, (2 + par) * QG:(3 + par) * QG],
                        cc == 0, cc == NCC - 1, waits=w1, inc=pe,
                        drain=(cc == NCC - 1))
                sq_free[par] = (pe, mt)
                sq_free[2 + par] = (pe, mt)
            vop("tensor_scalar", stats[0:1, 0:QG], ps_e[0:1, 0:QG], 1.0 / C, None, ALU.mult,
                waits=[(pe, mt)])
            vop("tensor_mul", stats[0:1, 512:1024], stats[0:1, 0:QG], stats[0:1, 0:QG])
            vfin = vop("scalar_tensor_tensor", stats[0:1, 512:1024], ps_c[0:1, 0:QG],
                       1.0 / C, stats[0:1, 512:1024], ALU.mult, ALU.subtract, inc=ve)
            brd("e", (ve, vfin))
            brd("c", (ve, vfin))
            aln = act(stats[0:1, 1024:1536], stats[0:1, 512:1024], AF.Ln, bias=1e-5,
                      waits=[(ve, vfin)], inc=ac)
            aex = act(stats[0:1, 512:1024], stats[0:1, 1024:1536], AF.Exp, scale=-0.5,
                      waits=[(ac, aln)], inc=ac)
            bmv = mm(ps_c[:, 0:QG], oner[0:1, :], stats[0:1, 0:QG], True, True,
                     waits=[(ve, vfin)] + bwt("c"), inc=pe)
            brv = mm(ps_d[:, 0:QG], oner[0:1, :], stats[0:1, 512:1024], True, True,
                     waits=[(ac, aex)] + bwt("d"), inc=pe)
            c0 = vop("tensor_copy", bc_sb[:, 0:QG], ps_c[:, 0:QG], waits=[(pe, bmv)], inc=ve)
            c1 = vop("tensor_copy", bc_sb[:, QG:2 * QG], ps_d[:, 0:QG], waits=[(pe, brv)], inc=ve)
            brd("c", (ve, c0))
            brd("d", (ve, c1))
            return (ve, c1), x_reads

        xn_state = {"call": 0, "pending": None}

        def emit_xn(bc_t):
            t = None
            for cc in range(NCC):
                par = cc % 2
                vop("tensor_sub", sil_sb[:, par * QG:(par + 1) * QG], xck(cc), bc_sb[:, 0:QG],
                    waits=[bc_t, xn_state["pending"] if cc == 0 else None])
                t = vop("tensor_mul", xnck(cc), sil_sb[:, par * QG:(par + 1) * QG],
                        bc_sb[:, QG:2 * QG], inc=ve)
            return (ve, t)

        def emit_routing(xn_t):
            lmt = None
            for tt in range(4):
                for cc in range(NCC):
                    w = [xn_t] + ([(gd, t_gwt)] + bwt("e") if (tt == 0 and cc == 0) else [])
                    lmt = mm(ps_e[:, tt * 8:(tt + 1) * 8],
                             xn_sb[:, cc * QG + tt * 128:cc * QG + (tt + 1) * 128],
                             gate_wT[:, cc * NE:(cc + 1) * NE],
                             cc == 0, cc == NCC - 1,
                             waits=w, inc=pe if cc == NCC - 1 else None)
            sgt = act(rt[:, 0:32], ps_e[:, 0:32], AF.Sigmoid,
                      waits=[(pe, lmt), route_state["pending"]], inc=ac)
            brd("e", (ac, sgt))

            def rvop(name, *args, waits=(), inc=None):
                if DBG_ROUTE_SYNC:
                    w = list(waits) + [(ve, ve.n, True)]
                    return vop(name, *args, waits=w, inc=ve)
                return vop(name, *args, waits=waits, inc=inc)
            S, SC, G, GR, GS, M8, S4, ER, ES, TW = 0, 32, 64, 80, 96, 112, 144, 176, 208, 240
            DEN, DINV, TMP = 272, 280, 288
            rvop("tensor_add", rt[:, SC:SC + 32], rt[:, S:S + 32], corr32[:],
                waits=[(ac, sgt), corr_done])
            scv = rt[:, SC:SC + 32].rearrange("p (t e) -> p t e", t=4)
            gv = rt[:, G:G + 16].rearrange("p (t g) -> p t g", t=4)
            rvop("tensor_tensor", gv, scv[:, :, 0:8:2], scv[:, :, 1:8:2], ALU.add)
            rvop("memset", rt[:, GR:GR + 16], 0.0)
            grv = rt[:, GR:GR + 16].rearrange("p (t g) -> p t g", t=4)
            tmpv = rt[:, TMP:TMP + 32].rearrange("p (t g) -> p t g", t=4)
            for d in range(1, 4):
                w = 4 - d
                rvop("tensor_tensor", tmpv[:, :, 0:w], gv[:, :, 0:w], gv[:, :, d:4], ALU.is_ge)
                rvop("tensor_tensor", grv[:, :, d:4], grv[:, :, d:4], tmpv[:, :, 0:w], ALU.add)
                rvop("tensor_tensor", tmpv[:, :, 0:w], gv[:, :, d:4], gv[:, :, 0:w], ALU.is_gt)
                rvop("tensor_tensor", grv[:, :, 0:w], grv[:, :, 0:w], tmpv[:, :, 0:w], ALU.add)
            rvop("tensor_scalar", rt[:, GS:GS + 16], rt[:, GR:GR + 16], 2.0, None, ALU.is_lt)
            gsv = rt[:, GS:GS + 16].rearrange("p (t g) -> p t g", t=4)
            m8v = rt[:, M8:M8 + 32].rearrange("p (t e) -> p t e", t=4)
            rvop("tensor_copy", m8v[:, :, 0:8:2], gsv)
            rvop("tensor_copy", m8v[:, :, 1:8:2], gsv)
            rvop("tensor_mul", rt[:, S4:S4 + 32], rt[:, M8:M8 + 32], rt[:, SC:SC + 32])
            rvop("memset", rt[:, ER:ER + 32], 0.0)
            s4v = rt[:, S4:S4 + 32].rearrange("p (t e) -> p t e", t=4)
            erv = rt[:, ER:ER + 32].rearrange("p (t e) -> p t e", t=4)
            tmv = rt[:, TMP:TMP + 64].rearrange("p (t e) -> p t e", t=8)[:, 0:4, :]
            for d in range(1, 8):
                w = 8 - d
                rvop("tensor_tensor", tmv[:, :, 0:w], s4v[:, :, 0:w], s4v[:, :, d:8], ALU.is_ge)
                rvop("tensor_tensor", erv[:, :, d:8], erv[:, :, d:8], tmv[:, :, 0:w], ALU.add)
                rvop("tensor_tensor", tmv[:, :, 0:w], s4v[:, :, d:8], s4v[:, :, 0:w], ALU.is_gt)
                rvop("tensor_tensor", erv[:, :, 0:w], erv[:, :, 0:w], tmv[:, :, 0:w], ALU.add)
            rvop("tensor_scalar", rt[:, ES:ES + 32], rt[:, ER:ER + 32], 4.0, None, ALU.is_lt)
            rvop("tensor_mul", rt[:, TW:TW + 32], rt[:, ES:ES + 32], rt[:, S:S + 32])
            twv = rt[:, TW:TW + 32].rearrange("p (t e) -> p t e", t=4)
            rvop("tensor_reduce", rt[:, DEN:DEN + 4], twv, AX.X, ALU.add)
            rvop("tensor_scalar", rt[:, DEN:DEN + 4], rt[:, DEN:DEN + 4], 1e-20, None, ALU.add)
            rvop("reciprocal", rt[:, DINV:DINV + 4], rt[:, DEN:DEN + 4])
            rvop("tensor_scalar", rt[:, DINV:DINV + 4], rt[:, DINV:DINV + 4], 1.25, None, ALU.mult)
            cw_t = None
            for tt in range(4):
                cw_t = rvop("tensor_scalar", cw_sb[:, tt * 8:(tt + 1) * 8],
                           rt[:, TW + tt * 8:TW + (tt + 1) * 8],
                           rt[:, DINV + tt:DINV + tt + 1], None, ALU.mult,
                           waits=[route_state["pending"]] if tt == 0 else (),
                           inc=ve)
            return (ve, cw_t)

        ffn_state = {"sf": slot_free, "call": 0}
        route_state = {"call": 0, "pending": None}

        def emit_ffn(xn_t, cw_t, x_writable):
            call = ffn_state["call"]
            ffn_state["call"] += 1
            sf = ffn_state["sf"]
            load_tgt = {}

            def ringload(i):
                s = i % NSLOT
                w = [sf[s]]
                if call == 0:
                    kind, e = morder[i]
                    tl = dma(gd, w_sb[:, s * WCOLS:(s + 1) * WCOLS], wsrc(kind, e),
                             waits=w)
                    dma(gd, wbf[i], w_sb[:, s * WCOLS:(s + 1) * WCOLS],
                        waits=[(gd, tl)])
                    load_tgt[i] = tl
                else:
                    load_tgt[i] = dma(gd, w_sb[:, s * WCOLS:(s + 1) * WCOLS], wbf[i],
                                      waits=w)

            x_upd = []
            gpar_free = [None, None]
            upar_free = [None, None]
            sil_free = [None, None]
            htm_free = [None, None]
            h_free = [None, None, None]
            last_gu = None
            for gi, grp in enumerate(EGROUPS):
                if DBG_GROUPS is not None and gi not in DBG_GROUPS:
                    continue
                h_evac = []
                for ei, e in enumerate(grp):
                    ig, iu = midx[("g", e)], midx[("u", e)]
                    ringload(ig)
                    ringload(iu)
                    sg, su = ig % NSLOT, iu % NSLOT
                    wgs = w_sb[:, sg * WCOLS:(sg + 1) * WCOLS]
                    wus = w_sb[:, su * WCOLS:(su + 1) * WCOLS]
                    for half2 in range(2):
                        for tt in (2 * half2, 2 * half2 + 1):
                            par = tt % 2
                            pg = ps_a[:, par * 512:(par + 1) * 512]
                            pu = ps_b[:, par * 512:(par + 1) * 512]
                            kg, ku = "a%d" % par, "b%d" % par
                            for cc in range(NCC):
                                w = []
                                if cc == 0:
                                    w = [(gd, load_tgt[ig]), xn_t, gpar_free[par]] + bwt(kg)
                                gmt = mm(pg, xn_sb[:, cc * QG + tt * 128:cc * QG + (tt + 1) * 128],
                                         wgs[:, cc * 512:(cc + 1) * 512], cc == 0, cc == NCC - 1,
                                         waits=w,
                                         inc=pe if cc == NCC - 1 else None, drain=False)
                            for cc in range(NCC):
                                w = []
                                if cc == 0:
                                    w = [(gd, load_tgt[iu]), upar_free[par]] + bwt(ku)
                                umt = mm(pu, xn_sb[:, cc * QG + tt * 128:cc * QG + (tt + 1) * 128],
                                         wus[:, cc * 512:(cc + 1) * 512], cc == 0, cc == NCC - 1,
                                         waits=w,
                                         inc=pe if cc == NCC - 1 else None, drain=False)
                            last_gu = umt
                        pair_tok = pe_drain_inc()
                        for tt in (2 * half2, 2 * half2 + 1):
                            par = tt % 2
                            pg = ps_a[:, par * 512:(par + 1) * 512]
                            pu = ps_b[:, par * 512:(par + 1) * 512]
                            kg, ku = "a%d" % par, "b%d" % par
                            silt = act(sil_sb[:, par * QG:(par + 1) * QG], pg, AF.Silu,
                                       waits=[(pe, pair_tok), sil_free[par]], inc=ac)
                            gpar_free[par] = (ac, silt)
                            brd(kg, (ac, silt))
                            scalar_ap = C05 if e == NE else cw_sb[:, tt * 8 + e:tt * 8 + e + 1]
                            htt = vop("scalar_tensor_tensor", htm_sb[:, par * QG:(par + 1) * QG],
                                      pu, scalar_ap, sil_sb[:, par * QG:(par + 1) * QG],
                                      ALU.mult, ALU.mult,
                                      waits=[(pe, pair_tok), (ac, silt), cw_t, htm_free[par]], inc=ve)
                            upar_free[par] = (ve, htt)
                            brd(ku, (ve, htt))
                            sil_free[par] = (ve, htt)
                            tpt = None
                            for ic in range(4):
                                w = [(ve, htt)] + (bwt("t") if ic == 0 else [])
                                tpt = trans(ps_t[:, ic * 128:(ic + 1) * 128],
                                            htm_sb[:, par * QG + ic * 128:par * QG + (ic + 1) * 128],
                                            idb[:], waits=w, inc=pe if ic == 3 else None)
                            htm_free[par] = (pe, tpt)
                            base = ei * 4 * QG + tt * 128
                            dstv = h_sb[:, base:base + 2048].rearrange("p (ic t) -> p ic t", ic=4)[:, :, 0:128]
                            hev = vop("tensor_copy", dstv,
                                      ps_t[:, 0:512].rearrange("p (ic t) -> p ic t", ic=4),
                                      waits=[(pe, tpt), h_free[ei] if tt == 0 else None], inc=ve)
                            brd("t", (ve, hev))
                            h_evac.append((ve, hev))
                    sf[ig % NSLOT] = (pe, gmt)
                    sf[iu % NSLOT] = (pe, umt)
                for e in grp:
                    ringload(midx[("d", e)])
                dmt_last = None
                for cc in range(NCC):
                    par = cc % 2
                    pd_ = ps_c if par == 0 else ps_d
                    kk = "c" if par == 0 else "d"
                    first = True
                    dmt = None
                    for ei, e in enumerate(grp):
                        idd = midx[("d", e)]
                        sdt = idd % NSLOT
                        wds = w_sb[:, sdt * WCOLS:(sdt + 1) * WCOLS]
                        for ic in range(4):
                            w = [(gd, load_tgt[idd])]
                            if first:
                                w = list(h_evac) + w + bwt(kk)
                            dmt = mm(pd_[:, 0:512],
                                     wds[:, ic * C + cc * 128:ic * C + (cc + 1) * 128],
                                     h_sb[:, ei * 4 * QG + ic * QG:ei * 4 * QG + (ic + 1) * QG],
                                     first, (ei == len(grp) - 1 and ic == 3),
                                     waits=w, inc=pe if (ei == len(grp) - 1 and ic == 3) else None)
                            first = False
                    xu = vop("tensor_tensor", xck(cc), xck(cc), pd_[:, 0:512], ALU.add,
                             waits=[(pe, dmt)] + list(x_writable), inc=ve)
                    brd(kk, (ve, xu))
                    x_upd.append((ve, xu))
                    dmt_last = dmt
                for ei in range(len(grp)):
                    h_free[ei] = (pe, dmt_last)
                for e in grp:
                    sf[midx[("d", e)] % NSLOT] = (ve, x_upd[-1][1])
            return x_upd[-NCC:], [(pe, last_gu)]

        stage_free = [vwT_ev, vwT_ev]
        x_free = [hm_done, corr_done, ob_done, vb_done, vwT_ev, (pe, tpc)]
        xn_free_g = []
        lat_last = None
        v_last = None
        bounce_tgt = {}

        for qg in range(DBG_NQG):
            x_evac_last = None
            for tt in range(4):
                sl = tt % 2
                tld = dma(sd, stage[:, sl * C:(sl + 1) * C],
                          x_in[qg * QG + tt * 128:qg * QG + (tt + 1) * 128, :],
                          waits=[stage_free[sl]])
                for half in range(2):
                    pd_ = ps_c if half == 0 else ps_d
                    kk = "c" if half == 0 else "d"
                    tpt = None
                    for j in range(4):
                        cc = half * 4 + j
                        w2 = [(sd, tld)]
                        if j == 0:
                            w2 += bwt(kk)
                        if tt == 0 and cc == 0:
                            w2 += x_free
                        tpt = trans(pd_[:, j * 128:(j + 1) * 128],
                                    stage[:, sl * C + cc * 128:sl * C + (cc + 1) * 128],
                                    idf[:], waits=w2, inc=pe if j == 3 else None)
                    base = half * 4 * QG + tt * 128
                    dstv = x_sb[:, base:base + 2048].rearrange("p (c t) -> p c t", c=4)[:, :, 0:128]
                    ev = vop("tensor_copy", dstv,
                             pd_[:, 0:512].rearrange("p (c t) -> p c t", c=4),
                             waits=[(pe, tpt)], inc=ve)
                    brd(kk, (ve, ev))
                    x_evac_last = (ve, ev)
                stage_free[sl] = (pe, tpt)

            if STAGE >= 2:
                bc_t, x_reads = emit_ln(x_evac_last)
                xn_t = emit_xn(bc_t)
                if DBG_NO_ROUTE:
                    zcw = vop("memset", cw_sb[:], 0.0, inc=ve)
                    cw_t = (ve, zcw)
                else:
                    cw_t = emit_routing(xn_t)
                if DBG_NO_FFN:
                    x_upd = [x_evac_last, cw_t, xn_t]
                else:
                    x_upd, xn_free_g = emit_ffn(xn_t, cw_t, [x_evac_last] + xn_free_g)
            else:
                x_upd, x_reads = [x_evac_last], []

            bt = dma(sd, x1s.rearrange("p (c t) -> p c t", c=NCC)[:, :, qg * QG:(qg + 1) * QG],
                     x_sb[:, 0:NCC * QG].rearrange("p (c t) -> p c t", c=NCC),
                     waits=list(x_upd))
            bounce_tgt[qg] = bt
            x_free = [(sd, bt)] + x_reads

            if STAGE >= 3:
                bc_t2, x_reads2 = emit_ln(x_upd[-1])
                xn_t2 = emit_xn(bc_t2)
                x_free = x_free + x_reads2
                lmt2 = None
                for cc in range(NCC):
                    w = [xn_t2] + ([(gd, t_bwt)] + bwt("c") if cc == 0 else [])
                    lmt2 = mm(ps_c[0:64, 0:512], basis_wT[:, cc * R:(cc + 1) * R],
                              xnck(cc), cc == 0, cc == NCC - 1, waits=w,
                              inc=pe if cc == NCC - 1 else None)
                lc = vop("tensor_copy", latT2[0:64, qg * QG:(qg + 1) * QG], ps_c[0:64, 0:512],
                         waits=[(pe, lmt2)], inc=ve)
                brd("c", (ve, lc))
                ld2 = dma(sd, latT2[64:128, qg * QG:(qg + 1) * QG],
                          latT2[0:64, qg * QG:(qg + 1) * QG], waits=[(ve, lc)])
                lat_last = (sd, ld2)
                vmt = None
                vev = None
                for tt in range(4):
                    for dh in range(2):
                        pv = ps_d if dh == 0 else ps_e
                        kk = "d" if dh == 0 else "e"
                        for cc in range(NCC):
                            w = [xn_t2] + (bwt(kk) if cc == 0 else [])
                            vmt = mm(pv[:, 0:512],
                                     xn_sb[:, cc * QG + tt * 128:cc * QG + (tt + 1) * 128],
                                     wT_sb[:, cc * C + dh * 512:cc * C + (dh + 1) * 512],
                                     cc == 0, cc == NCC - 1, waits=w,
                                     inc=pe if cc == NCC - 1 else None)
                        uc = qg * 4 + tt
                        dst = v_sb[:, uc * VSTRIDE + dh * 8 * 65:uc * VSTRIDE + (dh + 1) * 8 * 65] \
                            .rearrange("p (h d) -> p h d", h=8)[:, :, 0:64]
                        src = pv[:, 0:512].rearrange("p (h d) -> p h d", h=8)
                        vbv = vb_bc[:, dh * 512:(dh + 1) * 512].rearrange("p (h d) -> p h d", h=8)
                        vev = vop("tensor_tensor", dst, src, vbv, ALU.add,
                                  waits=[(pe, vmt), vb_done], inc=ve)
                        brd(kk, (ve, vev))
                v_last = (ve, vev)
                xn_free_g = [(pe, vmt)]

        if STAGE >= 3:
            owT_last = None
            for cc in range(NCC):
                owT_last = dma(gd, wT_sb[:, cc * C:(cc + 1) * C],
                               o_wt[cc * 128:(cc + 1) * 128, :],
                               waits=(xn_free_g + [v_last]) if cc == 0 else [])
            owt_ready = (gd, owT_last)
        else:
            owt_ready = None

        exp_free = [None, None]
        lt_free = [None, None]
        ybst_free = [None, None]
        for qg in range(DBG_NQG):
            tld = dma(sd, x_sb[:, 0:NCC * QG].rearrange("p (c t) -> p c t", c=NCC),
                      x1s.rearrange("p (c t) -> p c t", c=NCC)[:, :, qg * QG:(qg + 1) * QG],
                      waits=x_free + [(sd, bounce_tgt[qg])])
            x_ready = (sd, tld)
            x_upd2 = [x_ready]

            if STAGE >= 3:
                nuc = 4 * (qg + 1)
                lt_dma = {}

                def emit_ltT(p):
                    lpar = p % 2
                    hA, hB = 2 * p, 2 * p + 1
                    w = [lat_last, hm_done] + bwt("e")
                    la = mm(ps_e[0:64, 0:512], hm2[0:64, hA * R:(hA + 1) * R],
                            latT2[0:64, qg * QG:(qg + 1) * QG], True, True, tp=(0, 0),
                            waits=w, inc=pe)
                    e0 = vop("tensor_copy", ltT_sb[0:64, lpar * QG:(lpar + 1) * QG],
                             ps_e[0:64, 0:512], waits=[(pe, la), lt_free[lpar]], inc=ve)
                    lb = mm(ps_e[0:64, 0:512], hm2[0:64, hB * R:(hB + 1) * R],
                            latT2[0:64, qg * QG:(qg + 1) * QG], True, True, tp=(0, 0),
                            waits=[(ve, e0)], inc=pe)
                    e1 = vop("tensor_copy", ybst[:, lpar * 512:(lpar + 1) * 512],
                             ps_e[0:64, 0:512], waits=[(pe, lb), ybst_free[lpar]], inc=ve)
                    brd("e", (ve, e1))
                    dmt = dma(sd, ltT_sb[64:128, lpar * QG:(lpar + 1) * QG],
                              ybst[:, lpar * 512:(lpar + 1) * 512], waits=[(ve, e1)])
                    ybst_free[lpar] = (sd, dmt)
                    lt_dma[p] = [(ve, e0), (sd, dmt)]

                emit_ltT(0)
                for hp in range(NH // 2):
                    lpar = hp % 2
                    hA, hB = 2 * hp, 2 * hp + 1
                    lt_ready = lt_dma[hp]
                    avB = None
                    for j in range(nuc):
                        par = j % 2
                        psc = ps_a if par == 0 else ps_b
                        k0, k1 = ("a0", "a1") if par == 0 else ("b0", "b1")
                        sa = mm(psc[:, 0:512], latT2[0:64, j * 128:(j + 1) * 128],
                                ltT_sb[0:64, lpar * QG:(lpar + 1) * QG], True, True,
                                tp=(0, 0), waits=lt_ready + [v_last] + bwt(k0))
                        sb2 = mm(psc[:, 512:1024], latT2[64:128, j * 128:(j + 1) * 128],
                                 ltT_sb[64:128, lpar * QG:(lpar + 1) * QG], True, True,
                                 tp=(64, 0), waits=bwt(k1), inc=pe)
                        ext = act(exp_sb[:, par * 1024:(par + 1) * 1024], psc[:, 0:1024],
                                  AF.Exp, scale=0.125,
                                  waits=[(pe, sb2), exp_free[par]], inc=ac)
                        brd(k0, (ac, ext))
                        brd(k1, (ac, ext))
                        esrc = exp_sb[:, par * 1024:(par + 1) * 1024]
                        delta = j * 128 - qg * QG
                        if delta >= 0:
                            msl = mask_sb[:, 384 - delta:896 - delta]
                            vop("tensor_tensor", esrc[:, 0:512], esrc[:, 0:512],
                                msl, ALU.mult, waits=[(ac, ext)])
                            emk = vop("tensor_tensor", esrc[:, 512:1024], esrc[:, 512:1024],
                                      msl, ALU.mult, inc=ve)
                            eready = (ve, emk)
                        else:
                            eready = (ac, ext)
                        wA = [eready] + (bwt("c") if j == 0 else [])
                        mm(ps_c[0:65, 0:512],
                           v_sb[:, j * VSTRIDE + hA * 65:j * VSTRIDE + (hA + 1) * 65],
                           esrc[:, 0:512], j == 0, j == nuc - 1, waits=wA)
                        wB = bwt("d") if j == 0 else []
                        avB = mm(ps_d[0:65, 0:512],
                                 v_sb[:, j * VSTRIDE + hB * 65:j * VSTRIDE + (hB + 1) * 65],
                                 esrc[:, 512:1024], j == 0, j == nuc - 1, waits=wB, inc=pe)
                        exp_free[par] = (pe, avB)
                        if j == 0 and hp + 1 < NH // 2:
                            emit_ltT(hp + 1)
                    av_done = (pe, avB)
                    lt_free[lpar] = av_done
                    r1 = vop("reciprocal", scr_f32[64:65, 512:1024], ps_c[64:65, 0:512],
                             waits=[av_done], inc=ve)
                    ba = mm(ps_e[:, 0:512], oner[64:65, :], scr_f32[64:65, 512:1024],
                            True, True, tp=(64, 0), waits=[(ve, r1)] + bwt("e"), inc=pe)
                    b0 = vop("tensor_copy", scr_f32[:, 0:512], ps_e[:, 0:512],
                             waits=[(pe, ba)], inc=ve)
                    brd("e", (ve, b0))
                    ya = vop("tensor_tensor", yT_sb[0:64, hp * QG:(hp + 1) * QG],
                             ps_c[0:64, 0:512], scr_f32[0:64, 0:512], ALU.mult,
                             waits=[(ve, b0)] + bwt("yt"), inc=ve)
                    brd("c", (ve, ya))
                    r2 = vop("reciprocal", scr_f32[64:65, 512:1024], ps_d[64:65, 0:512],
                             waits=[(pe, ba), av_done], inc=ve)
                    bb = mm(ps_e[:, 0:512], oner[64:65, :], scr_f32[64:65, 512:1024],
                            True, True, tp=(64, 0), waits=[(ve, r2), (ve, b0)], inc=pe)
                    b1 = vop("tensor_copy", scr_f32[:, 0:512], ps_e[:, 0:512],
                             waits=[(pe, bb), (ve, ya)], inc=ve)
                    brd("e", (ve, b1))
                    yb = vop("tensor_tensor", ybst[:, lpar * 512:(lpar + 1) * 512],
                             ps_d[0:64, 0:512], scr_f32[0:64, 0:512], ALU.mult,
                             waits=[(ve, b1), ybst_free[lpar]], inc=ve)
                    brd("d", (ve, yb))
                    ydm = dma(sd, yT_sb[64:128, hp * QG:(hp + 1) * QG],
                              ybst[:, lpar * 512:(lpar + 1) * 512],
                              waits=[(ve, yb)] + bwt("yt"))
                    ybst_free[lpar] = (sd, ydm)
                    x_upd2.append((sd, ydm))
                yt_dma_done = x_upd2[-1]
                x_upd2 = [x_ready]
                bank.pop("yt", None)
                omt = None
                for cc in range(NCC):
                    par = cc % 2
                    po = ps_c if par == 0 else ps_d
                    kk = "c" if par == 0 else "d"
                    for dc in range(NCC):
                        w = []
                        if dc == 0:
                            w = [yt_dma_done, owt_ready] + bwt(kk)
                        omt = mm(po[:, 0:512], wT_sb[:, dc * C + cc * 128:dc * C + (cc + 1) * 128],
                                 yT_sb[:, dc * QG:(dc + 1) * QG], dc == 0, dc == NCC - 1,
                                 waits=w, inc=pe if dc == NCC - 1 else None)
                    vop("tensor_tensor", xck(cc), xck(cc), po[:, 0:512], ALU.add,
                        waits=[(pe, omt), x_ready, ob_done])
                    xu = vop("tensor_scalar", xck(cc), xck(cc), ob_col[:, cc:cc + 1],
                             None, ALU.add, inc=ve)
                    brd(kk, (ve, xu))
                    x_upd2.append((ve, xu))
                brd("yt", (pe, omt))
                x_upd2 = x_upd2[-NCC:]

            if STAGE >= 4:
                bc_t3, _xr3 = emit_ln(x_upd2[-1])
                xn_t3 = emit_xn(bc_t3)
                if DBG_NO_ROUTE:
                    zcw3 = vop("memset", cw_sb[:], 0.0, inc=ve)
                    cw_t3 = (ve, zcw3)
                else:
                    cw_t3 = emit_routing(xn_t3)
                if not DBG_NO_FFN:
                    x_upd2, xn_free_g = emit_ffn(xn_t3, cw_t3, x_upd2)

            x_free = []
            for tt in range(4):
                sl = tt % 2
                ev_t = None
                for half in range(2):
                    pd_ = ps_c if half == 0 else ps_d
                    kk = "c" if half == 0 else "d"
                    tpt = None
                    for j in range(4):
                        cc = half * 4 + j
                        w2 = []
                        if j == 0:
                            w2 = bwt(kk)
                        if tt == 0 and cc == 0:
                            w2 = w2 + list(x_upd2)
                        tpt = trans(pd_[:, j * 128:(j + 1) * 128],
                                    x_sb[:, cc * QG + tt * 128:cc * QG + (tt + 1) * 128],
                                    idf[:], waits=w2, inc=pe if j == 3 else None)
                    ev_t = vop("tensor_copy",
                               stage[:, sl * C + half * 512:sl * C + (half + 1) * 512],
                               pd_[:, 0:512],
                               waits=[(pe, tpt), stage_free[sl] if half == 0 else None], inc=ve)
                    brd(kk, (ve, ev_t))
                tst = dma(sd, out_hbm[qg * QG + tt * 128:qg * QG + (tt + 1) * 128, :],
                          stage[:, sl * C:(sl + 1) * C], waits=[(ve, ev_t)])
                stage_free[sl] = (sd, tst)
                x_free = [(pe, tpt)]

        prog.emit("sync", lambda e, n=sd.n: e.wait_ge(sd.h, n))
        prog.emit("gpsimd", lambda e, n=gd.n: e.wait_ge(gd.h, n))

        for eng in ENGS:
            fns = prog.th[eng]
            getattr(blk, eng)(lambda e, fns=fns: [f(e) for f in fns])

    return nc


_NC_CACHE = {}


def _get_nc():
    if "nc" not in _NC_CACHE:
        _NC_CACHE["nc"] = build_program()
    return _NC_CACHE["nc"]


def kernel(L0_attn_hidden_states, L0_attn_basis_w, L0_attn_core,
           L0_attn_head_residual, L0_attn_v_w, L0_attn_v_b, L0_attn_o_w,
           L0_attn_o_b, L0_ffn_gate_w, L0_ffn_correction_bias, L0_ffn_Wg,
           L0_ffn_Wu, L0_ffn_Wd, L0_ffn_Wg_s, L0_ffn_Wu_s, L0_ffn_Wd_s):
    x = np.ascontiguousarray(np.asarray(L0_attn_hidden_states, dtype=np.float32))

    def f32(a):
        return np.ascontiguousarray(np.asarray(a, dtype=np.float32))

    eye = np.eye(128, dtype=np.float32)
    pp = np.arange(128)[:, None]
    qq = np.arange(896)[None, :]
    masks = (pp + 384 <= qq).astype(BF)
    consts = np.tile(np.array([CONSTS], np.float32), (128, 1))

    shared = {
        "basis_wt": f32(np.asarray(L0_attn_basis_w, np.float32).T),
        "core_w": f32(L0_attn_core),
        "head_res": f32(L0_attn_head_residual),
        "v_wt": f32(np.asarray(L0_attn_v_w, np.float32).T),
        "o_wt": f32(np.asarray(L0_attn_o_w, np.float32).T),
        "gate_wt": f32(np.asarray(L0_ffn_gate_w, np.float32).T),
        "corr_bc": np.tile(f32(L0_ffn_correction_bias).reshape(1, NE), (128, 4)),
        "vb_bc": np.tile(f32(L0_attn_v_b).reshape(1, C), (128, 1)),
        "ob_pc": f32(np.asarray(L0_attn_o_b, np.float32).reshape(NCC, 128).T),
        "Wg": f32(L0_ffn_Wg),
        "Wu": f32(L0_ffn_Wu),
        "Wdm": f32(L0_ffn_Wd),
        "Wg_s": f32(L0_ffn_Wg_s),
        "Wu_s": f32(L0_ffn_Wu_s),
        "Wd_s": f32(L0_ffn_Wd_s),
        "idf": eye,
        "idb": eye.astype(BF),
        "ones_c": np.ones((128, 1), BF),
        "ones_r": np.ones((65, 128), np.float32),
        "masks": masks,
        "consts": consts,
    }
    in_maps = []
    for b in range(8):
        m = dict(shared)
        m["x_in"] = np.ascontiguousarray(x[b])
        in_maps.append(m)

    nc = _get_nc()
    res = run_bass_kernel_spmd(nc, in_maps, core_ids=list(range(8)))
    out = np.stack([np.asarray(r["out"], dtype=np.float32) for r in res.results], axis=0)
    return out


# version 07
# speedup vs baseline: 1.0190x; 1.0190x over previous
import numpy as np
import ml_dtypes
import concourse.bass as bass
import concourse.mybir as mybir
from concourse.bass_utils import run_bass_kernel_spmd
from contextlib import ExitStack

F32 = mybir.dt.float32
BF16 = mybir.dt.bfloat16
AF = mybir.ActivationFunctionType
ALU = mybir.AluOpType
AX = mybir.AxisListType
BF = ml_dtypes.bfloat16

T, C, NH, R, NE, DI = 4096, 1024, 16, 64, 8, 512
QG = 512
NQG = T // QG
NCC = C // 128
NUC = T // 128
EGROUPS = [(0, 1, 2), (3, 4, 5), (6, 7, 8)]
NSLOT = 3
WCOLS = 4096
VSTRIDE = NH * 65

STAGE = 4
DBG_GROUPS = None
DBG_NO_CONV = False
DBG_NO_PREP = False
DBG_PREP_LEVEL = 4
DBG_NO_MEMSET = False
DBG_NQG = NQG
DBG_DMA_CHAIN = False
DBG_NO_FFN = False
DBG_NO_ROUTE = False
DBG_ROUTE_SYNC = True
DBG_SELF_SYNC = False
DBG_PE_FENCE = False
DBG_XFENCE = False
DBG_WARMUP = False
DBG_DRAIN = True

CONSTS = [1.0 / C, 1e-5, 1e-20, 1.25, 2.0, 4.0, 0.5, 0.03125, 1.0 / NH]
ENGS = ("sync", "gpsimd", "tensor", "scalar", "vector")


class Sem:
    __slots__ = ("h", "n", "owner", "dma")

    def __init__(self, h, owner, dma=False):
        self.h, self.n, self.owner, self.dma = h, 0, owner, dma

    def inc(self, k=1):
        self.n += k
        return self.n


class Prog:
    def __init__(self):
        self.th = {e: [] for e in ENGS}
        self.waited = {e: {} for e in ENGS}

    def emit(self, eng, op, waits=(), inc=None):
        w = []
        for it in waits:
            if it is None:
                continue
            s, v = it[0], it[1]
            if s is None or v is None or v <= 0:
                continue
            force = len(it) == 3 and it[2]
            if s.owner == eng and not s.dma and not force:
                continue
            if self.waited[eng].get(id(s), -1) >= v:
                continue
            self.waited[eng][id(s)] = v
            w.append((s, v))

        def f(e, op=op, w=w, inc=inc):
            for s, v in w:
                e.wait_ge(s.h, v)
            ins = op(e)
            if inc is not None:
                ins.then_inc(inc[0].h, inc[1])
        self.th[eng].append(f)


def build_program():
    nc = bass.Bass("TRN2", target_bir_lowering=False, detect_race_conditions=False)

    x_in = nc.dram_tensor("x_in", [T, C], F32, kind="ExternalInput").ap()
    basis_wt = nc.dram_tensor("basis_wt", [C, R], F32, kind="ExternalInput").ap()
    core_w = nc.dram_tensor("core_w", [R, R], F32, kind="ExternalInput").ap()
    head_res = nc.dram_tensor("head_res", [NH, R, R], F32, kind="ExternalInput").ap()
    v_wt = nc.dram_tensor("v_wt", [C, C], F32, kind="ExternalInput").ap()
    o_wt = nc.dram_tensor("o_wt", [C, C], F32, kind="ExternalInput").ap()
    gate_wt = nc.dram_tensor("gate_wt", [C, NE], F32, kind="ExternalInput").ap()
    corr_bc_in = nc.dram_tensor("corr_bc", [128, 32], F32, kind="ExternalInput").ap()
    vb_bc_in = nc.dram_tensor("vb_bc", [128, C], F32, kind="ExternalInput").ap()
    ob_pc_in = nc.dram_tensor("ob_pc", [128, NCC], F32, kind="ExternalInput").ap()
    Wg = nc.dram_tensor("Wg", [NE, C, DI], F32, kind="ExternalInput").ap()
    Wu = nc.dram_tensor("Wu", [NE, C, DI], F32, kind="ExternalInput").ap()
    Wdm = nc.dram_tensor("Wdm", [NE, DI, C], F32, kind="ExternalInput").ap()
    Wg_s = nc.dram_tensor("Wg_s", [C, DI], F32, kind="ExternalInput").ap()
    Wu_s = nc.dram_tensor("Wu_s", [C, DI], F32, kind="ExternalInput").ap()
    Wd_s = nc.dram_tensor("Wd_s", [DI, C], F32, kind="ExternalInput").ap()
    idf_in = nc.dram_tensor("idf", [128, 128], F32, kind="ExternalInput").ap()
    idb_in = nc.dram_tensor("idb", [128, 128], BF16, kind="ExternalInput").ap()
    ones_c = nc.dram_tensor("ones_c", [128, 1], BF16, kind="ExternalInput").ap()
    ones_r = nc.dram_tensor("ones_r", [65, 128], F32, kind="ExternalInput").ap()
    masks_in = nc.dram_tensor("masks", [128, 896], BF16, kind="ExternalInput").ap()
    consts_in = nc.dram_tensor("consts", [128, len(CONSTS)], F32, kind="ExternalInput").ap()
    out_hbm = nc.dram_tensor("out", [T, C], F32, kind="ExternalOutput").ap()
    x1s = nc.dram_tensor("x1s", [128, NCC * T], F32, kind="Internal").ap()
    wbf = nc.dram_tensor("wbf", [27, 128, WCOLS], BF16, kind="Internal").ap()

    prog = Prog()

    with ExitStack() as ctx:
        En = ctx.enter_context
        v_sb = En(nc.sbuf_tensor("v_sb", [128, NUC * VSTRIDE], BF16))
        latT2 = En(nc.sbuf_tensor("latT2", [128, T], BF16))
        wT_sb = En(nc.sbuf_tensor("wT_sb", [128, NCC * C], BF16))
        basis_wT = En(nc.sbuf_tensor("basis_wT", [128, NCC * R], BF16))
        gate_wT = En(nc.sbuf_tensor("gate_wT", [128, NCC * NE], BF16))
        hm2 = En(nc.sbuf_tensor("hm2", [128, NH * R], BF16))
        scr_f32 = En(nc.sbuf_tensor("scr_f32", [128, C], F32))
        vb_bc = scr_f32
        ob_col = En(nc.sbuf_tensor("ob_col", [128, NCC], F32))
        corr32 = En(nc.sbuf_tensor("corr32", [128, 32], F32))
        idf = En(nc.sbuf_tensor("idf_sb", [128, 128], F32))
        idb = En(nc.sbuf_tensor("idb_sb", [128, 128], BF16))
        onec = En(nc.sbuf_tensor("onec", [128, 1], BF16))
        oner = En(nc.sbuf_tensor("oner", [65, 128], F32))
        mask_sb = En(nc.sbuf_tensor("mask_sb", [128, 896], BF16))
        c_sb = En(nc.sbuf_tensor("c_sb", [128, len(CONSTS)], F32))
        stats = En(nc.sbuf_tensor("stats", [1, 1536], F32))
        fence_sb = En(nc.sbuf_tensor("fence_sb", [1, 8], F32))
        rt = En(nc.sbuf_tensor("rt", [128, 352], F32))
        cw_sb = En(nc.sbuf_tensor("cw_sb", [128, 32], F32))
        ybst = En(nc.sbuf_tensor("ybst", [64, 1024], BF16))
        stage = En(nc.sbuf_tensor("stage", [128, 2 * C], F32))
        x_sb = En(nc.sbuf_tensor("x_sb", [128, NCC * QG + 384], F32))
        xn_sb = En(nc.sbuf_tensor("xn_sb", [128, NCC * QG], BF16))
        h_sb = En(nc.sbuf_tensor("h_sb", [128, 3 * 4 * QG + 384], BF16))
        w_sb = En(nc.sbuf_tensor("w_sb", [128, NSLOT * WCOLS], BF16))
        sil_sb = En(nc.sbuf_tensor("sil_sb", [128, 2 * QG], F32))
        sq_sb = En(nc.sbuf_tensor("sq_sb", [128, 4 * QG], BF16))
        bc_sb = En(nc.sbuf_tensor("bc_sb", [128, 2 * QG], F32))
        htm_sb = En(nc.sbuf_tensor("htm_sb", [128, 2 * QG], BF16))
        exp_sb = En(nc.sbuf_tensor("exp_sb", [128, 2 * 1024], BF16))
        yT_sb = En(nc.sbuf_tensor("yT_sb", [128, NCC * QG], BF16))
        ltT_sb = En(nc.sbuf_tensor("ltT_sb", [128, 2 * QG], BF16))
        ps_a = En(nc.psum_tensor([128, 1024], F32))
        ps_b = En(nc.psum_tensor([128, 1024], F32))
        ps_c = En(nc.psum_tensor([128, 512], F32))
        ps_d = En(nc.psum_tensor([128, 512], F32))
        ps_e = En(nc.psum_tensor([128, 512], F32))
        ps_t = En(nc.psum_tensor([128, 1024], BF16))

        for ci, cv in enumerate(CONSTS):
            nc.const_aps.register(F32, cv, c_sb[:, ci:ci + 1])
        C05 = c_sb[:, CONSTS.index(0.5):CONSTS.index(0.5) + 1]

        sd = Sem(En(nc.semaphore("sd")), "sync", dma=True)
        gd = Sem(En(nc.semaphore("gd")), "gpsimd", dma=True)
        pe = Sem(En(nc.semaphore("pe")), "tensor")
        ac = Sem(En(nc.semaphore("ac")), "scalar")
        ve = Sem(En(nc.semaphore("ve")), "vector")
        blk = En(nc.Block())

        SELFSEM = {"scalar": ac, "vector": ve}

        def E(eng, op, waits=(), inc=None, k=1, drain=True):
            explicit = inc is not None
            if DBG_SELF_SYNC and eng in SELFSEM:
                sem = SELFSEM[eng]
                waits = list(waits) + [(sem, sem.n, True)]
                if inc is None:
                    inc = sem
            if DBG_DRAIN and explicit and drain and eng in ("tensor", "scalar", "vector"):
                prog.emit(eng, op, waits, None)
                tgt = inc.inc(k)

                def dop(e):
                    return e.drain()
                prog.emit(eng, dop, (), (inc, k))
                return tgt
            tgt = None
            if inc is not None:
                tgt = inc.inc(k)
            prog.emit(eng, op, waits, (inc, k) if inc is not None else None)
            return tgt

        def pe_drain_inc():
            def dop(e):
                return e.drain()
            tgt = pe.inc(1)
            prog.emit(eng="tensor", op=dop, waits=(), inc=(pe, 1))
            return tgt

        def dma(eng_sem, dst, src, waits=()):
            eng = "sync" if eng_sem is sd else "gpsimd"
            if DBG_DMA_CHAIN and eng_sem.n > 0:
                waits = list(waits) + [(eng_sem, eng_sem.n)]

            def op(e, dst=dst, src=src):
                return e.dma_start(dst, src)
            return E(eng, op, waits, eng_sem, 16)

        def pe_fence():
            def op(t):
                return t.transpose(ps_t[0:1, 768:769], onec[0:1, 0:1], idb[0:1, 0:1])
            return E("tensor", op, (), pe)

        def mm(out, lhsT, rhs, start, stop, tp=None, waits=(), inc=None, drain=True):
            def op(t, out=out, lhsT=lhsT, rhs=rhs, start=start, stop=stop, tp=tp):
                return t.matmul(out, lhsT, rhs, start=start, stop=stop, tile_position=tp)
            if (DBG_PE_FENCE or DBG_XFENCE) and not DBG_DRAIN and inc is pe:
                E("tensor", op, waits, None)
                return pe_fence()
            return E("tensor", op, waits, inc, drain=drain)

        def trans(out, in_, ident, waits=(), inc=None):
            def op(t, out=out, in_=in_, ident=ident):
                return t.transpose(out, in_, ident)
            if (DBG_PE_FENCE or DBG_XFENCE) and not DBG_DRAIN and inc is pe:
                E("tensor", op, waits, None)
                return pe_fence()
            return E("tensor", op, waits, inc)

        def act(out, in_, func, bias=0.0, scale=1.0, waits=(), inc=None):
            def op(s, out=out, in_=in_, func=func, bias=bias, scale=scale):
                return s.activation(out, in_, func, bias=bias, scale=scale)
            if DBG_XFENCE and not DBG_DRAIN and inc is not None:
                E("scalar", op, waits, None)

                def fop(s, func=func):
                    return s.activation(fence_sb[0:1, 2:3], c_sb[0:1, 0:1], func)
                return E("scalar", fop, (), inc)
            return E("scalar", op, waits, inc)

        def vop(name, *args, waits=(), inc=None):
            def op(v, name=name, args=args):
                return getattr(v, name)(*args)
            if DBG_XFENCE and not DBG_DRAIN and inc is not None:
                E("vector", op, waits, None)

                def fop(v):
                    return v.tensor_copy(fence_sb[0:1, 0:1], fence_sb[0:1, 1:2])
                return E("vector", fop, (), inc)
            return E("vector", op, waits, inc)

        bank = {}

        def brd(key, tgt):
            bank.setdefault(key, {})[id(tgt[0])] = tgt

        def bwt(key):
            return list(bank.get(key, {}).values())

        dma(sd, idf[:], idf_in)
        dma(sd, idb[:], idb_in)
        dma(sd, onec[:], ones_c)
        dma(sd, oner[:], ones_r)
        dma(sd, mask_sb[:], masks_in)
        t_c = dma(sd, c_sb[:], consts_in)
        t_bwt = dma(gd, basis_wT[:].rearrange("p (cc r) -> p cc r", cc=NCC),
                    basis_wt.rearrange("(cc p) r -> p cc r", p=128))
        t_gwt = dma(gd, gate_wT[:].rearrange("p (cc e) -> p cc e", cc=NCC),
                    gate_wt.rearrange("(cc p) e -> p cc e", p=128))
        t_corr = dma(sd, corr32[:], corr_bc_in)
        t_vbb = dma(sd, vb_bc[:, 0:C], vb_bc_in)
        t_ob = dma(sd, ob_col[:], ob_pc_in)
        t_hr = dma(sd, x_sb[0:64, 0:NH * R], head_res.rearrange("h r s -> r h s"))
        t_core = dma(sd, x_sb[0:64, 1024:1024 + R], core_w)
        vop("memset", fence_sb[0:1, 0:8], 1.0)
        for uc in range(NUC if not DBG_NO_MEMSET else 0):
            vop("memset",
                v_sb[:, uc * VSTRIDE:(uc + 1) * VSTRIDE]
                .rearrange("p (h d) -> p h d", h=NH)[:, :, 64:65], 1.0)

        corr_done = (sd, t_corr)
        vb_done = (sd, t_vbb)
        ob_done = (sd, t_ob)
        tpc = trans(ps_e[0:64, 0:64], x_sb[0:64, 1024:1024 + R], idf[0:64, 0:64],
                    waits=[(sd, t_core)], inc=pe)
        ce = vop("tensor_copy", x_sb[0:64, 1088:1088 + R], ps_e[0:64, 0:64],
                 waits=[(pe, tpc)], inc=ve)
        brd("e", (ve, ce))
        vop("tensor_add", x_sb[0:64, 1152:1216], x_sb[0:64, 1024:1088], x_sb[0:64, 1088:1152])
        vop("tensor_scalar", x_sb[0:64, 1152:1216], x_sb[0:64, 1152:1216], 0.03125, None, ALU.mult,
            waits=[(sd, t_c)])
        vop("tensor_add", x_sb[0:64, 1280:1792], x_sb[0:64, 0:512], x_sb[0:64, 512:1024],
            waits=[(sd, t_hr)])
        vop("tensor_add", x_sb[0:64, 1792:2048], x_sb[0:64, 1280:1536], x_sb[0:64, 1536:1792])
        vop("tensor_add", x_sb[0:64, 2048:2176], x_sb[0:64, 1792:1920], x_sb[0:64, 1920:2048])
        vop("tensor_add", x_sb[0:64, 1216:1280], x_sb[0:64, 2048:2112], x_sb[0:64, 2112:2176])
        vop("tensor_scalar", x_sb[0:64, 1216:1280], x_sb[0:64, 1216:1280], 1.0 / NH, None, ALU.mult)
        vop("tensor_sub", x_sb[0:64, 1216:1280], x_sb[0:64, 1216:1280], x_sb[0:64, 1152:1216])
        for h in range(NH):
            vop("tensor_sub", x_sb[0:64, 2176 + h * R:2176 + (h + 1) * R],
                x_sb[0:64, h * R:(h + 1) * R], x_sb[0:64, 1216:1280])
        hm_lo = vop("tensor_copy", hm2[0:64, :], x_sb[0:64, 2176:2176 + NH * R], inc=ve)
        hm_hi = dma(sd, hm2[64:128, :], hm2[0:64, :], waits=[(ve, hm_lo)])
        hm_done = (sd, hm_hi)

        vwT_last = None
        for cc in range(NCC):
            vwT_last = dma(gd, wT_sb[:, cc * C:(cc + 1) * C],
                           v_wt[cc * 128:(cc + 1) * 128, :])
        vwT_ev = (gd, vwT_last)

        morder = []
        for grp in EGROUPS:
            for e in grp:
                morder.append(("g", e))
                morder.append(("u", e))
            for e in grp:
                morder.append(("d", e))
        midx = {m: i for i, m in enumerate(morder)}

        def wsrc(kind, e):
            if kind == "g":
                ap = Wg[e] if e < NE else Wg_s
                return ap.rearrange("(k p) i -> p k i", p=128)
            if kind == "u":
                ap = Wu[e] if e < NE else Wu_s
                return ap.rearrange("(k p) i -> p k i", p=128)
            ap = Wdm[e] if e < NE else Wd_s
            return ap.rearrange("(k p) c -> p k c", p=128)

        slot_free = [None] * NSLOT

        def xck(cc):
            return x_sb[:, cc * QG:(cc + 1) * QG]

        def xnck(cc):
            return xn_sb[:, cc * QG:(cc + 1) * QG]

        def emit_ln(x_ready):
            sq_free = [None] * 4
            mt = None
            x_reads = []
            for cc in range(NCC):
                par = cc % 2
                cb = vop("tensor_copy", sq_sb[:, par * QG:(par + 1) * QG], xck(cc),
                         waits=[x_ready, sq_free[par]], inc=ve)
                sqt = act(sq_sb[:, (2 + par) * QG:(3 + par) * QG], xck(cc), AF.Square,
                          waits=[x_ready, sq_free[2 + par]], inc=ac)
                x_reads = [(ve, cb), (ac, sqt)]
                w0 = [(ve, cb)] + (bwt("e") if cc == 0 else [])
                mm(ps_e[0:1, 0:QG], onec[:], sq_sb[:, par * QG:(par + 1) * QG],
                   cc == 0, cc == NCC - 1, waits=w0)
                w1 = [(ac, sqt)] + (bwt("c") if cc == 0 else [])
                mt = mm(ps_c[0:1, 0:QG], onec[:], sq_sb[:, (2 + par) * QG:(3 + par) * QG],
                        cc == 0, cc == NCC - 1, waits=w1, inc=pe,
                        drain=(cc == NCC - 1))
                sq_free[par] = (pe, mt)
                sq_free[2 + par] = (pe, mt)
            vop("tensor_scalar", stats[0:1, 0:QG], ps_e[0:1, 0:QG], 1.0 / C, None, ALU.mult,
                waits=[(pe, mt)])
            vop("tensor_mul", stats[0:1, 512:1024], stats[0:1, 0:QG], stats[0:1, 0:QG])
            vfin = vop("scalar_tensor_tensor", stats[0:1, 512:1024], ps_c[0:1, 0:QG],
                       1.0 / C, stats[0:1, 512:1024], ALU.mult, ALU.subtract, inc=ve)
            brd("e", (ve, vfin))
            brd("c", (ve, vfin))
            aln = act(stats[0:1, 1024:1536], stats[0:1, 512:1024], AF.Ln, bias=1e-5,
                      waits=[(ve, vfin)], inc=ac)
            aex = act(stats[0:1, 512:1024], stats[0:1, 1024:1536], AF.Exp, scale=-0.5,
                      waits=[(ac, aln)], inc=ac)
            bmv = mm(ps_c[:, 0:QG], oner[0:1, :], stats[0:1, 0:QG], True, True,
                     waits=[(ve, vfin)] + bwt("c"), inc=pe)
            brv = mm(ps_d[:, 0:QG], oner[0:1, :], stats[0:1, 512:1024], True, True,
                     waits=[(ac, aex)] + bwt("d"), inc=pe)
            c0 = vop("tensor_copy", bc_sb[:, 0:QG], ps_c[:, 0:QG], waits=[(pe, bmv)], inc=ve)
            c1 = vop("tensor_copy", bc_sb[:, QG:2 * QG], ps_d[:, 0:QG], waits=[(pe, brv)], inc=ve)
            brd("c", (ve, c0))
            brd("d", (ve, c1))
            return (ve, c1), x_reads

        xn_state = {"call": 0, "pending": None}

        def emit_xn(bc_t):
            t = None
            for cc in range(NCC):
                par = cc % 2
                vop("tensor_sub", sil_sb[:, par * QG:(par + 1) * QG], xck(cc), bc_sb[:, 0:QG],
                    waits=[bc_t, xn_state["pending"] if cc == 0 else None])
                t = vop("tensor_mul", xnck(cc), sil_sb[:, par * QG:(par + 1) * QG],
                        bc_sb[:, QG:2 * QG], inc=ve)
            return (ve, t)

        def emit_routing(xn_t):
            lmt = None
            for tt in range(4):
                for cc in range(NCC):
                    w = [xn_t] + ([(gd, t_gwt)] + bwt("e") if (tt == 0 and cc == 0) else [])
                    lmt = mm(ps_e[:, tt * 8:(tt + 1) * 8],
                             xn_sb[:, cc * QG + tt * 128:cc * QG + (tt + 1) * 128],
                             gate_wT[:, cc * NE:(cc + 1) * NE],
                             cc == 0, cc == NCC - 1,
                             waits=w, inc=pe if cc == NCC - 1 else None)
            sgt = act(rt[:, 0:32], ps_e[:, 0:32], AF.Sigmoid,
                      waits=[(pe, lmt), route_state["pending"]], inc=ac)
            brd("e", (ac, sgt))

            def rvop(name, *args, waits=(), inc=None):
                if DBG_ROUTE_SYNC:
                    w = list(waits) + [(ve, ve.n, True)]
                    return vop(name, *args, waits=w, inc=ve)
                return vop(name, *args, waits=waits, inc=inc)
            S, SC, G, GR, GS, M8, S4, ER, ES, TW = 0, 32, 64, 80, 96, 112, 144, 176, 208, 240
            DEN, DINV, TMP = 272, 280, 288
            rvop("tensor_add", rt[:, SC:SC + 32], rt[:, S:S + 32], corr32[:],
                waits=[(ac, sgt), corr_done])
            scv = rt[:, SC:SC + 32].rearrange("p (t e) -> p t e", t=4)
            gv = rt[:, G:G + 16].rearrange("p (t g) -> p t g", t=4)
            rvop("tensor_tensor", gv, scv[:, :, 0:8:2], scv[:, :, 1:8:2], ALU.add)
            rvop("memset", rt[:, GR:GR + 16], 0.0)
            grv = rt[:, GR:GR + 16].rearrange("p (t g) -> p t g", t=4)
            tmpv = rt[:, TMP:TMP + 32].rearrange("p (t g) -> p t g", t=4)
            for d in range(1, 4):
                w = 4 - d
                rvop("tensor_tensor", tmpv[:, :, 0:w], gv[:, :, 0:w], gv[:, :, d:4], ALU.is_ge)
                rvop("tensor_tensor", grv[:, :, d:4], grv[:, :, d:4], tmpv[:, :, 0:w], ALU.add)
                rvop("tensor_tensor", tmpv[:, :, 0:w], gv[:, :, d:4], gv[:, :, 0:w], ALU.is_gt)
                rvop("tensor_tensor", grv[:, :, 0:w], grv[:, :, 0:w], tmpv[:, :, 0:w], ALU.add)
            rvop("tensor_scalar", rt[:, GS:GS + 16], rt[:, GR:GR + 16], 2.0, None, ALU.is_lt)
            gsv = rt[:, GS:GS + 16].rearrange("p (t g) -> p t g", t=4)
            m8v = rt[:, M8:M8 + 32].rearrange("p (t e) -> p t e", t=4)
            rvop("tensor_copy", m8v[:, :, 0:8:2], gsv)
            rvop("tensor_copy", m8v[:, :, 1:8:2], gsv)
            rvop("tensor_mul", rt[:, S4:S4 + 32], rt[:, M8:M8 + 32], rt[:, SC:SC + 32])
            rvop("memset", rt[:, ER:ER + 32], 0.0)
            s4v = rt[:, S4:S4 + 32].rearrange("p (t e) -> p t e", t=4)
            erv = rt[:, ER:ER + 32].rearrange("p (t e) -> p t e", t=4)
            tmv = rt[:, TMP:TMP + 64].rearrange("p (t e) -> p t e", t=8)[:, 0:4, :]
            for d in range(1, 8):
                w = 8 - d
                rvop("tensor_tensor", tmv[:, :, 0:w], s4v[:, :, 0:w], s4v[:, :, d:8], ALU.is_ge)
                rvop("tensor_tensor", erv[:, :, d:8], erv[:, :, d:8], tmv[:, :, 0:w], ALU.add)
                rvop("tensor_tensor", tmv[:, :, 0:w], s4v[:, :, d:8], s4v[:, :, 0:w], ALU.is_gt)
                rvop("tensor_tensor", erv[:, :, 0:w], erv[:, :, 0:w], tmv[:, :, 0:w], ALU.add)
            rvop("tensor_scalar", rt[:, ES:ES + 32], rt[:, ER:ER + 32], 4.0, None, ALU.is_lt)
            rvop("tensor_mul", rt[:, TW:TW + 32], rt[:, ES:ES + 32], rt[:, S:S + 32])
            twv = rt[:, TW:TW + 32].rearrange("p (t e) -> p t e", t=4)
            rvop("tensor_reduce", rt[:, DEN:DEN + 4], twv, AX.X, ALU.add)
            rvop("tensor_scalar", rt[:, DEN:DEN + 4], rt[:, DEN:DEN + 4], 1e-20, None, ALU.add)
            rvop("reciprocal", rt[:, DINV:DINV + 4], rt[:, DEN:DEN + 4])
            rvop("tensor_scalar", rt[:, DINV:DINV + 4], rt[:, DINV:DINV + 4], 1.25, None, ALU.mult)
            cw_t = None
            for tt in range(4):
                cw_t = rvop("tensor_scalar", cw_sb[:, tt * 8:(tt + 1) * 8],
                           rt[:, TW + tt * 8:TW + (tt + 1) * 8],
                           rt[:, DINV + tt:DINV + tt + 1], None, ALU.mult,
                           waits=[route_state["pending"]] if tt == 0 else (),
                           inc=ve)
            return (ve, cw_t)

        ffn_state = {"sf": slot_free, "call": 0}
        route_state = {"call": 0, "pending": None}

        def emit_ffn(xn_t, cw_t, x_writable):
            call = ffn_state["call"]
            ffn_state["call"] += 1
            sf = ffn_state["sf"]
            load_tgt = {}

            def ringload(i):
                s = i % NSLOT
                w = [sf[s]]
                if call == 0:
                    kind, e = morder[i]
                    tl = dma(gd, w_sb[:, s * WCOLS:(s + 1) * WCOLS], wsrc(kind, e),
                             waits=w)
                    dma(gd, wbf[i], w_sb[:, s * WCOLS:(s + 1) * WCOLS],
                        waits=[(gd, tl)])
                    load_tgt[i] = tl
                else:
                    load_tgt[i] = dma(gd, w_sb[:, s * WCOLS:(s + 1) * WCOLS], wbf[i],
                                      waits=w)

            x_upd = []
            gpar_free = [None, None]
            upar_free = [None, None]
            sil_free = [None, None]
            htm_free = [None, None]
            h_free = [None, None, None]
            last_gu = None
            for gi, grp in enumerate(EGROUPS):
                if DBG_GROUPS is not None and gi not in DBG_GROUPS:
                    continue
                h_evac = []
                for ei, e in enumerate(grp):
                    ig, iu = midx[("g", e)], midx[("u", e)]
                    ringload(ig)
                    ringload(iu)
                    sg, su = ig % NSLOT, iu % NSLOT
                    wgs = w_sb[:, sg * WCOLS:(sg + 1) * WCOLS]
                    wus = w_sb[:, su * WCOLS:(su + 1) * WCOLS]
                    for tt in range(4):
                        par = tt % 2
                        pg = ps_a[:, par * 512:(par + 1) * 512]
                        pu = ps_b[:, par * 512:(par + 1) * 512]
                        kg, ku = "a%d" % par, "b%d" % par
                        gmt = umt = None
                        for cc in range(NCC):
                            w = []
                            if cc == 0:
                                w = [(gd, load_tgt[ig]), xn_t, gpar_free[par]] + bwt(kg)
                            gmt = mm(pg, xn_sb[:, cc * QG + tt * 128:cc * QG + (tt + 1) * 128],
                                     wgs[:, cc * 512:(cc + 1) * 512], cc == 0, cc == NCC - 1,
                                     waits=w, inc=pe if cc == NCC - 1 else None)
                        for cc in range(NCC):
                            w = []
                            if cc == 0:
                                w = [(gd, load_tgt[iu]), upar_free[par]] + bwt(ku)
                            umt = mm(pu, xn_sb[:, cc * QG + tt * 128:cc * QG + (tt + 1) * 128],
                                     wus[:, cc * 512:(cc + 1) * 512], cc == 0, cc == NCC - 1,
                                     waits=w, inc=pe if cc == NCC - 1 else None)
                        last_gu = umt
                        silt = act(sil_sb[:, par * QG:(par + 1) * QG], pg, AF.Silu,
                                   waits=[(pe, gmt), sil_free[par]], inc=ac)
                        gpar_free[par] = (ac, silt)
                        brd(kg, (ac, silt))
                        scalar_ap = C05 if e == NE else cw_sb[:, tt * 8 + e:tt * 8 + e + 1]
                        htt = vop("scalar_tensor_tensor", htm_sb[:, par * QG:(par + 1) * QG],
                                  pu, scalar_ap, sil_sb[:, par * QG:(par + 1) * QG],
                                  ALU.mult, ALU.mult,
                                  waits=[(pe, umt), (ac, silt), cw_t, htm_free[par]], inc=ve)
                        upar_free[par] = (ve, htt)
                        brd(ku, (ve, htt))
                        sil_free[par] = (ve, htt)
                        tpt = None
                        for ic in range(4):
                            w = [(ve, htt)] + (bwt("t") if ic == 0 else [])
                            tpt = trans(ps_t[:, ic * 128:(ic + 1) * 128],
                                        htm_sb[:, par * QG + ic * 128:par * QG + (ic + 1) * 128],
                                        idb[:], waits=w, inc=pe if ic == 3 else None)
                        htm_free[par] = (pe, tpt)
                        base = ei * 4 * QG + tt * 128
                        dstv = h_sb[:, base:base + 2048].rearrange("p (ic t) -> p ic t", ic=4)[:, :, 0:128]
                        hev = vop("tensor_copy", dstv,
                                  ps_t[:, 0:512].rearrange("p (ic t) -> p ic t", ic=4),
                                  waits=[(pe, tpt), h_free[ei] if tt == 0 else None], inc=ve)
                        brd("t", (ve, hev))
                        h_evac.append((ve, hev))
                    sf[ig % NSLOT] = (pe, gmt)
                    sf[iu % NSLOT] = (pe, umt)
                for e in grp:
                    ringload(midx[("d", e)])
                dmt_last = None
                for cc in range(NCC):
                    par = cc % 2
                    pd_ = ps_c if par == 0 else ps_d
                    kk = "c" if par == 0 else "d"
                    first = True
                    dmt = None
                    for ei, e in enumerate(grp):
                        idd = midx[("d", e)]
                        sdt = idd % NSLOT
                        wds = w_sb[:, sdt * WCOLS:(sdt + 1) * WCOLS]
                        for ic in range(4):
                            w = [(gd, load_tgt[idd])]
                            if first:
                                w = list(h_evac) + w + bwt(kk)
                            dmt = mm(pd_[:, 0:512],
                                     wds[:, ic * C + cc * 128:ic * C + (cc + 1) * 128],
                                     h_sb[:, ei * 4 * QG + ic * QG:ei * 4 * QG + (ic + 1) * QG],
                                     first, (ei == len(grp) - 1 and ic == 3),
                                     waits=w, inc=pe if (ei == len(grp) - 1 and ic == 3) else None)
                            first = False
                    xu = vop("tensor_tensor", xck(cc), xck(cc), pd_[:, 0:512], ALU.add,
                             waits=[(pe, dmt)] + list(x_writable), inc=ve)
                    brd(kk, (ve, xu))
                    x_upd.append((ve, xu))
                    dmt_last = dmt
                for ei in range(len(grp)):
                    h_free[ei] = (pe, dmt_last)
                for e in grp:
                    sf[midx[("d", e)] % NSLOT] = (ve, x_upd[-1][1])
            return x_upd[-NCC:], [(pe, last_gu)]

        stage_free = [vwT_ev, vwT_ev]
        x_free = [hm_done, corr_done, ob_done, vb_done, vwT_ev, (pe, tpc)]
        xn_free_g = []
        lat_last = None
        v_last = None
        bounce_tgt = {}

        for qg in range(DBG_NQG):
            x_evac_last = None
            for tt in range(4):
                sl = tt % 2
                tld = dma(sd, stage[:, sl * C:(sl + 1) * C],
                          x_in[qg * QG + tt * 128:qg * QG + (tt + 1) * 128, :],
                          waits=[stage_free[sl]])
                for half in range(2):
                    pd_ = ps_c if half == 0 else ps_d
                    kk = "c" if half == 0 else "d"
                    tpt = None
                    for j in range(4):
                        cc = half * 4 + j
                        w2 = [(sd, tld)]
                        if j == 0:
                            w2 += bwt(kk)
                        if tt == 0 and cc == 0:
                            w2 += x_free
                        tpt = trans(pd_[:, j * 128:(j + 1) * 128],
                                    stage[:, sl * C + cc * 128:sl * C + (cc + 1) * 128],
                                    idf[:], waits=w2, inc=pe if j == 3 else None)
                    base = half * 4 * QG + tt * 128
                    dstv = x_sb[:, base:base + 2048].rearrange("p (c t) -> p c t", c=4)[:, :, 0:128]
                    ev = vop("tensor_copy", dstv,
                             pd_[:, 0:512].rearrange("p (c t) -> p c t", c=4),
                             waits=[(pe, tpt)], inc=ve)
                    brd(kk, (ve, ev))
                    x_evac_last = (ve, ev)
                stage_free[sl] = (pe, tpt)

            if STAGE >= 2:
                bc_t, x_reads = emit_ln(x_evac_last)
                xn_t = emit_xn(bc_t)
                if DBG_NO_ROUTE:
                    zcw = vop("memset", cw_sb[:], 0.0, inc=ve)
                    cw_t = (ve, zcw)
                else:
                    cw_t = emit_routing(xn_t)
                if DBG_NO_FFN:
                    x_upd = [x_evac_last, cw_t, xn_t]
                else:
                    x_upd, xn_free_g = emit_ffn(xn_t, cw_t, [x_evac_last] + xn_free_g)
            else:
                x_upd, x_reads = [x_evac_last], []

            bt = dma(sd, x1s.rearrange("p (c t) -> p c t", c=NCC)[:, :, qg * QG:(qg + 1) * QG],
                     x_sb[:, 0:NCC * QG].rearrange("p (c t) -> p c t", c=NCC),
                     waits=list(x_upd))
            bounce_tgt[qg] = bt
            x_free = [(sd, bt)] + x_reads

            if STAGE >= 3:
                bc_t2, x_reads2 = emit_ln(x_upd[-1])
                xn_t2 = emit_xn(bc_t2)
                x_free = x_free + x_reads2
                lmt2 = None
                for cc in range(NCC):
                    w = [xn_t2] + ([(gd, t_bwt)] + bwt("c") if cc == 0 else [])
                    lmt2 = mm(ps_c[0:64, 0:512], basis_wT[:, cc * R:(cc + 1) * R],
                              xnck(cc), cc == 0, cc == NCC - 1, waits=w,
                              inc=pe if cc == NCC - 1 else None)
                lc = vop("tensor_copy", latT2[0:64, qg * QG:(qg + 1) * QG], ps_c[0:64, 0:512],
                         waits=[(pe, lmt2)], inc=ve)
                brd("c", (ve, lc))
                ld2 = dma(sd, latT2[64:128, qg * QG:(qg + 1) * QG],
                          latT2[0:64, qg * QG:(qg + 1) * QG], waits=[(ve, lc)])
                lat_last = (sd, ld2)
                vmt = None
                vev = None
                for tt in range(4):
                    for dh in range(2):
                        pv = ps_d if dh == 0 else ps_e
                        kk = "d" if dh == 0 else "e"
                        for cc in range(NCC):
                            w = [xn_t2] + (bwt(kk) if cc == 0 else [])
                            vmt = mm(pv[:, 0:512],
                                     xn_sb[:, cc * QG + tt * 128:cc * QG + (tt + 1) * 128],
                                     wT_sb[:, cc * C + dh * 512:cc * C + (dh + 1) * 512],
                                     cc == 0, cc == NCC - 1, waits=w,
                                     inc=pe if cc == NCC - 1 else None)
                        uc = qg * 4 + tt
                        dst = v_sb[:, uc * VSTRIDE + dh * 8 * 65:uc * VSTRIDE + (dh + 1) * 8 * 65] \
                            .rearrange("p (h d) -> p h d", h=8)[:, :, 0:64]
                        src = pv[:, 0:512].rearrange("p (h d) -> p h d", h=8)
                        vbv = vb_bc[:, dh * 512:(dh + 1) * 512].rearrange("p (h d) -> p h d", h=8)
                        vev = vop("tensor_tensor", dst, src, vbv, ALU.add,
                                  waits=[(pe, vmt), vb_done], inc=ve)
                        brd(kk, (ve, vev))
                v_last = (ve, vev)
                xn_free_g = [(pe, vmt)]

        if STAGE >= 3:
            owT_last = None
            for cc in range(NCC):
                owT_last = dma(gd, wT_sb[:, cc * C:(cc + 1) * C],
                               o_wt[cc * 128:(cc + 1) * 128, :],
                               waits=(xn_free_g + [v_last]) if cc == 0 else [])
            owt_ready = (gd, owT_last)
        else:
            owt_ready = None

        exp_free = [None, None]
        lt_free = [None, None]
        ybst_free = [None, None]
        for qg in range(DBG_NQG):
            tld = dma(sd, x_sb[:, 0:NCC * QG].rearrange("p (c t) -> p c t", c=NCC),
                      x1s.rearrange("p (c t) -> p c t", c=NCC)[:, :, qg * QG:(qg + 1) * QG],
                      waits=x_free + [(sd, bounce_tgt[qg])])
            x_ready = (sd, tld)
            x_upd2 = [x_ready]

            if STAGE >= 3:
                nuc = 4 * (qg + 1)
                lt_dma = {}

                def emit_ltT(p):
                    lpar = p % 2
                    hA, hB = 2 * p, 2 * p + 1
                    w = [lat_last, hm_done] + bwt("e")
                    la = mm(ps_e[0:64, 0:512], hm2[0:64, hA * R:(hA + 1) * R],
                            latT2[0:64, qg * QG:(qg + 1) * QG], True, True, tp=(0, 0),
                            waits=w, inc=pe)
                    e0 = vop("tensor_copy", ltT_sb[0:64, lpar * QG:(lpar + 1) * QG],
                             ps_e[0:64, 0:512], waits=[(pe, la), lt_free[lpar]], inc=ve)
                    lb = mm(ps_e[0:64, 0:512], hm2[0:64, hB * R:(hB + 1) * R],
                            latT2[0:64, qg * QG:(qg + 1) * QG], True, True, tp=(0, 0),
                            waits=[(ve, e0)], inc=pe)
                    e1 = vop("tensor_copy", ybst[:, lpar * 512:(lpar + 1) * 512],
                             ps_e[0:64, 0:512], waits=[(pe, lb), ybst_free[lpar]], inc=ve)
                    brd("e", (ve, e1))
                    dmt = dma(sd, ltT_sb[64:128, lpar * QG:(lpar + 1) * QG],
                              ybst[:, lpar * 512:(lpar + 1) * 512], waits=[(ve, e1)])
                    ybst_free[lpar] = (sd, dmt)
                    lt_dma[p] = [(ve, e0), (sd, dmt)]

                emit_ltT(0)
                for hp in range(NH // 2):
                    lpar = hp % 2
                    hA, hB = 2 * hp, 2 * hp + 1
                    lt_ready = lt_dma[hp]
                    avB = None
                    for j in range(nuc):
                        par = j % 2
                        psc = ps_a if par == 0 else ps_b
                        k0, k1 = ("a0", "a1") if par == 0 else ("b0", "b1")
                        sa = mm(psc[:, 0:512], latT2[0:64, j * 128:(j + 1) * 128],
                                ltT_sb[0:64, lpar * QG:(lpar + 1) * QG], True, True,
                                tp=(0, 0), waits=lt_ready + [v_last] + bwt(k0))
                        sb2 = mm(psc[:, 512:1024], latT2[64:128, j * 128:(j + 1) * 128],
                                 ltT_sb[64:128, lpar * QG:(lpar + 1) * QG], True, True,
                                 tp=(64, 0), waits=bwt(k1), inc=pe)
                        ext = act(exp_sb[:, par * 1024:(par + 1) * 1024], psc[:, 0:1024],
                                  AF.Exp, scale=0.125,
                                  waits=[(pe, sb2), exp_free[par]], inc=ac)
                        brd(k0, (ac, ext))
                        brd(k1, (ac, ext))
                        esrc = exp_sb[:, par * 1024:(par + 1) * 1024]
                        delta = j * 128 - qg * QG
                        if delta >= 0:
                            msl = mask_sb[:, 384 - delta:896 - delta]
                            vop("tensor_tensor", esrc[:, 0:512], esrc[:, 0:512],
                                msl, ALU.mult, waits=[(ac, ext)])
                            emk = vop("tensor_tensor", esrc[:, 512:1024], esrc[:, 512:1024],
                                      msl, ALU.mult, inc=ve)
                            eready = (ve, emk)
                        else:
                            eready = (ac, ext)
                        wA = [eready] + (bwt("c") if j == 0 else [])
                        mm(ps_c[0:65, 0:512],
                           v_sb[:, j * VSTRIDE + hA * 65:j * VSTRIDE + (hA + 1) * 65],
                           esrc[:, 0:512], j == 0, j == nuc - 1, waits=wA)
                        wB = bwt("d") if j == 0 else []
                        avB = mm(ps_d[0:65, 0:512],
                                 v_sb[:, j * VSTRIDE + hB * 65:j * VSTRIDE + (hB + 1) * 65],
                                 esrc[:, 512:1024], j == 0, j == nuc - 1, waits=wB, inc=pe)
                        exp_free[par] = (pe, avB)
                        if j == 0 and hp + 1 < NH // 2:
                            emit_ltT(hp + 1)
                    av_done = (pe, avB)
                    lt_free[lpar] = av_done
                    r1 = vop("reciprocal", scr_f32[64:65, 512:1024], ps_c[64:65, 0:512],
                             waits=[av_done], inc=ve)
                    ba = mm(ps_e[:, 0:512], oner[64:65, :], scr_f32[64:65, 512:1024],
                            True, True, tp=(64, 0), waits=[(ve, r1)] + bwt("e"), inc=pe)
                    b0 = vop("tensor_copy", scr_f32[:, 0:512], ps_e[:, 0:512],
                             waits=[(pe, ba)], inc=ve)
                    brd("e", (ve, b0))
                    ya = vop("tensor_tensor", yT_sb[0:64, hp * QG:(hp + 1) * QG],
                             ps_c[0:64, 0:512], scr_f32[0:64, 0:512], ALU.mult,
                             waits=[(ve, b0)] + bwt("yt"), inc=ve)
                    brd("c", (ve, ya))
                    r2 = vop("reciprocal", scr_f32[64:65, 512:1024], ps_d[64:65, 0:512],
                             waits=[(pe, ba), av_done], inc=ve)
                    bb = mm(ps_e[:, 0:512], oner[64:65, :], scr_f32[64:65, 512:1024],
                            True, True, tp=(64, 0), waits=[(ve, r2), (ve, b0)], inc=pe)
                    b1 = vop("tensor_copy", scr_f32[:, 0:512], ps_e[:, 0:512],
                             waits=[(pe, bb), (ve, ya)], inc=ve)
                    brd("e", (ve, b1))
                    yb = vop("tensor_tensor", ybst[:, lpar * 512:(lpar + 1) * 512],
                             ps_d[0:64, 0:512], scr_f32[0:64, 0:512], ALU.mult,
                             waits=[(ve, b1), ybst_free[lpar]], inc=ve)
                    brd("d", (ve, yb))
                    ydm = dma(sd, yT_sb[64:128, hp * QG:(hp + 1) * QG],
                              ybst[:, lpar * 512:(lpar + 1) * 512],
                              waits=[(ve, yb)] + bwt("yt"))
                    ybst_free[lpar] = (sd, ydm)
                    x_upd2.append((sd, ydm))
                yt_dma_done = x_upd2[-1]
                x_upd2 = [x_ready]
                bank.pop("yt", None)
                omt = None
                for cc in range(NCC):
                    par = cc % 2
                    po = ps_c if par == 0 else ps_d
                    kk = "c" if par == 0 else "d"
                    for dc in range(NCC):
                        w = []
                        if dc == 0:
                            w = [yt_dma_done, owt_ready] + bwt(kk)
                        omt = mm(po[:, 0:512], wT_sb[:, dc * C + cc * 128:dc * C + (cc + 1) * 128],
                                 yT_sb[:, dc * QG:(dc + 1) * QG], dc == 0, dc == NCC - 1,
                                 waits=w, inc=pe if dc == NCC - 1 else None)
                    vop("tensor_tensor", xck(cc), xck(cc), po[:, 0:512], ALU.add,
                        waits=[(pe, omt), x_ready, ob_done])
                    xu = vop("tensor_scalar", xck(cc), xck(cc), ob_col[:, cc:cc + 1],
                             None, ALU.add, inc=ve)
                    brd(kk, (ve, xu))
                    x_upd2.append((ve, xu))
                brd("yt", (pe, omt))
                x_upd2 = x_upd2[-NCC:]

            if STAGE >= 4:
                bc_t3, _xr3 = emit_ln(x_upd2[-1])
                xn_t3 = emit_xn(bc_t3)
                if DBG_NO_ROUTE:
                    zcw3 = vop("memset", cw_sb[:], 0.0, inc=ve)
                    cw_t3 = (ve, zcw3)
                else:
                    cw_t3 = emit_routing(xn_t3)
                if not DBG_NO_FFN:
                    x_upd2, xn_free_g = emit_ffn(xn_t3, cw_t3, x_upd2)

            x_free = []
            for tt in range(4):
                sl = tt % 2
                ev_t = None
                for half in range(2):
                    pd_ = ps_c if half == 0 else ps_d
                    kk = "c" if half == 0 else "d"
                    tpt = None
                    for j in range(4):
                        cc = half * 4 + j
                        w2 = []
                        if j == 0:
                            w2 = bwt(kk)
                        if tt == 0 and cc == 0:
                            w2 = w2 + list(x_upd2)
                        tpt = trans(pd_[:, j * 128:(j + 1) * 128],
                                    x_sb[:, cc * QG + tt * 128:cc * QG + (tt + 1) * 128],
                                    idf[:], waits=w2, inc=pe if j == 3 else None)
                    ev_t = vop("tensor_copy",
                               stage[:, sl * C + half * 512:sl * C + (half + 1) * 512],
                               pd_[:, 0:512],
                               waits=[(pe, tpt), stage_free[sl] if half == 0 else None], inc=ve)
                    brd(kk, (ve, ev_t))
                tst = dma(sd, out_hbm[qg * QG + tt * 128:qg * QG + (tt + 1) * 128, :],
                          stage[:, sl * C:(sl + 1) * C], waits=[(ve, ev_t)])
                stage_free[sl] = (sd, tst)
                x_free = [(pe, tpt)]

        prog.emit("sync", lambda e, n=sd.n: e.wait_ge(sd.h, n))
        prog.emit("gpsimd", lambda e, n=gd.n: e.wait_ge(gd.h, n))

        for eng in ENGS:
            fns = prog.th[eng]
            getattr(blk, eng)(lambda e, fns=fns: [f(e) for f in fns])

    return nc


_NC_CACHE = {}


def _get_nc():
    if "nc" not in _NC_CACHE:
        _NC_CACHE["nc"] = build_program()
    return _NC_CACHE["nc"]


def kernel(L0_attn_hidden_states, L0_attn_basis_w, L0_attn_core,
           L0_attn_head_residual, L0_attn_v_w, L0_attn_v_b, L0_attn_o_w,
           L0_attn_o_b, L0_ffn_gate_w, L0_ffn_correction_bias, L0_ffn_Wg,
           L0_ffn_Wu, L0_ffn_Wd, L0_ffn_Wg_s, L0_ffn_Wu_s, L0_ffn_Wd_s):
    x = np.ascontiguousarray(np.asarray(L0_attn_hidden_states, dtype=np.float32))

    def f32(a):
        return np.ascontiguousarray(np.asarray(a, dtype=np.float32))

    eye = np.eye(128, dtype=np.float32)
    pp = np.arange(128)[:, None]
    qq = np.arange(896)[None, :]
    masks = (pp + 384 <= qq).astype(BF)
    consts = np.tile(np.array([CONSTS], np.float32), (128, 1))

    shared = {
        "basis_wt": f32(np.asarray(L0_attn_basis_w, np.float32).T),
        "core_w": f32(L0_attn_core),
        "head_res": f32(L0_attn_head_residual),
        "v_wt": f32(np.asarray(L0_attn_v_w, np.float32).T),
        "o_wt": f32(np.asarray(L0_attn_o_w, np.float32).T),
        "gate_wt": f32(np.asarray(L0_ffn_gate_w, np.float32).T),
        "corr_bc": np.tile(f32(L0_ffn_correction_bias).reshape(1, NE), (128, 4)),
        "vb_bc": np.tile(f32(L0_attn_v_b).reshape(1, C), (128, 1)),
        "ob_pc": f32(np.asarray(L0_attn_o_b, np.float32).reshape(NCC, 128).T),
        "Wg": f32(L0_ffn_Wg),
        "Wu": f32(L0_ffn_Wu),
        "Wdm": f32(L0_ffn_Wd),
        "Wg_s": f32(L0_ffn_Wg_s),
        "Wu_s": f32(L0_ffn_Wu_s),
        "Wd_s": f32(L0_ffn_Wd_s),
        "idf": eye,
        "idb": eye.astype(BF),
        "ones_c": np.ones((128, 1), BF),
        "ones_r": np.ones((65, 128), np.float32),
        "masks": masks,
        "consts": consts,
    }
    in_maps = []
    for b in range(8):
        m = dict(shared)
        m["x_in"] = np.ascontiguousarray(x[b])
        in_maps.append(m)

    nc = _get_nc()
    res = run_bass_kernel_spmd(nc, in_maps, core_ids=list(range(8)))
    out = np.stack([np.asarray(r["out"], dtype=np.float32) for r in res.results], axis=0)
    return out


# version 08
# speedup vs baseline: 1.0239x; 1.0048x over previous
import numpy as np
import ml_dtypes
import concourse.bass as bass
import concourse.mybir as mybir
from concourse.bass_utils import run_bass_kernel_spmd
from contextlib import ExitStack

F32 = mybir.dt.float32
BF16 = mybir.dt.bfloat16
AF = mybir.ActivationFunctionType
ALU = mybir.AluOpType
AX = mybir.AxisListType
BF = ml_dtypes.bfloat16

T, C, NH, R, NE, DI = 4096, 1024, 16, 64, 8, 512
QG = 512
NQG = T // QG
NCC = C // 128
NUC = T // 128
EGROUPS = [(0, 1, 2), (3, 4, 5), (6, 7, 8)]
NSLOT = 3
WCOLS = 4096
VSTRIDE = NH * 65

STAGE = 4
DBG_GROUPS = None
DBG_NO_CONV = False
DBG_NO_PREP = False
DBG_PREP_LEVEL = 4
DBG_NO_MEMSET = False
DBG_NQG = NQG
DBG_DMA_CHAIN = False
DBG_NO_FFN = False
DBG_NO_ROUTE = False
DBG_ROUTE_SYNC = True
DBG_SELF_SYNC = False
DBG_PE_FENCE = False
DBG_XFENCE = False
DBG_WARMUP = False
DBG_DRAIN = True

CONSTS = [1.0 / C, 1e-5, 1e-20, 1.25, 2.0, 4.0, 0.5, 0.03125, 1.0 / NH]
ENGS = ("sync", "gpsimd", "tensor", "scalar", "vector")


class Sem:
    __slots__ = ("h", "n", "owner", "dma")

    def __init__(self, h, owner, dma=False):
        self.h, self.n, self.owner, self.dma = h, 0, owner, dma

    def inc(self, k=1):
        self.n += k
        return self.n


class Prog:
    def __init__(self):
        self.th = {e: [] for e in ENGS}
        self.waited = {e: {} for e in ENGS}

    def emit(self, eng, op, waits=(), inc=None):
        w = []
        for it in waits:
            if it is None:
                continue
            s, v = it[0], it[1]
            if s is None or v is None or v <= 0:
                continue
            force = len(it) == 3 and it[2]
            if s.owner == eng and not s.dma and not force:
                continue
            if self.waited[eng].get(id(s), -1) >= v:
                continue
            self.waited[eng][id(s)] = v
            w.append((s, v))

        def f(e, op=op, w=w, inc=inc):
            for s, v in w:
                e.wait_ge(s.h, v)
            ins = op(e)
            if inc is not None:
                ins.then_inc(inc[0].h, inc[1])
        self.th[eng].append(f)


def build_program():
    nc = bass.Bass("TRN2", target_bir_lowering=False, detect_race_conditions=False)

    x_in = nc.dram_tensor("x_in", [T, C], F32, kind="ExternalInput").ap()
    basis_wt = nc.dram_tensor("basis_wt", [C, R], F32, kind="ExternalInput").ap()
    core_w = nc.dram_tensor("core_w", [R, R], F32, kind="ExternalInput").ap()
    head_res = nc.dram_tensor("head_res", [NH, R, R], F32, kind="ExternalInput").ap()
    v_wt = nc.dram_tensor("v_wt", [C, C], F32, kind="ExternalInput").ap()
    o_wt = nc.dram_tensor("o_wt", [C, C], F32, kind="ExternalInput").ap()
    gate_wt = nc.dram_tensor("gate_wt", [C, NE], F32, kind="ExternalInput").ap()
    corr_bc_in = nc.dram_tensor("corr_bc", [128, 32], F32, kind="ExternalInput").ap()
    vb_bc_in = nc.dram_tensor("vb_bc", [128, C], F32, kind="ExternalInput").ap()
    ob_pc_in = nc.dram_tensor("ob_pc", [128, NCC], F32, kind="ExternalInput").ap()
    Wg = nc.dram_tensor("Wg", [NE, C, DI], F32, kind="ExternalInput").ap()
    Wu = nc.dram_tensor("Wu", [NE, C, DI], F32, kind="ExternalInput").ap()
    Wdm = nc.dram_tensor("Wdm", [NE, DI, C], F32, kind="ExternalInput").ap()
    Wg_s = nc.dram_tensor("Wg_s", [C, DI], F32, kind="ExternalInput").ap()
    Wu_s = nc.dram_tensor("Wu_s", [C, DI], F32, kind="ExternalInput").ap()
    Wd_s = nc.dram_tensor("Wd_s", [DI, C], F32, kind="ExternalInput").ap()
    idf_in = nc.dram_tensor("idf", [128, 128], F32, kind="ExternalInput").ap()
    idb_in = nc.dram_tensor("idb", [128, 128], BF16, kind="ExternalInput").ap()
    ones_c = nc.dram_tensor("ones_c", [128, 1], BF16, kind="ExternalInput").ap()
    ones_r = nc.dram_tensor("ones_r", [65, 128], F32, kind="ExternalInput").ap()
    masks_in = nc.dram_tensor("masks", [128, 896], BF16, kind="ExternalInput").ap()
    consts_in = nc.dram_tensor("consts", [128, len(CONSTS)], F32, kind="ExternalInput").ap()
    out_hbm = nc.dram_tensor("out", [T, C], F32, kind="ExternalOutput").ap()
    x1s = nc.dram_tensor("x1s", [128, NCC * T], F32, kind="Internal").ap()
    wbf = nc.dram_tensor("wbf", [27, 128, WCOLS], BF16, kind="Internal").ap()

    prog = Prog()

    with ExitStack() as ctx:
        En = ctx.enter_context
        v_sb = En(nc.sbuf_tensor("v_sb", [128, NUC * VSTRIDE], BF16))
        latT2 = En(nc.sbuf_tensor("latT2", [128, T], BF16))
        wT_sb = En(nc.sbuf_tensor("wT_sb", [128, NCC * C], BF16))
        basis_wT = En(nc.sbuf_tensor("basis_wT", [128, NCC * R], BF16))
        gate_wT = En(nc.sbuf_tensor("gate_wT", [128, NCC * NE], BF16))
        hm2 = En(nc.sbuf_tensor("hm2", [128, NH * R], BF16))
        scr_f32 = En(nc.sbuf_tensor("scr_f32", [128, C], F32))
        vb_bc = scr_f32
        ob_col = En(nc.sbuf_tensor("ob_col", [128, NCC], F32))
        corr32 = En(nc.sbuf_tensor("corr32", [128, 32], F32))
        idf = En(nc.sbuf_tensor("idf_sb", [128, 128], F32))
        idb = En(nc.sbuf_tensor("idb_sb", [128, 128], BF16))
        onec = En(nc.sbuf_tensor("onec", [128, 1], BF16))
        oner = En(nc.sbuf_tensor("oner", [65, 128], F32))
        mask_sb = En(nc.sbuf_tensor("mask_sb", [128, 896], BF16))
        c_sb = En(nc.sbuf_tensor("c_sb", [128, len(CONSTS)], F32))
        stats = En(nc.sbuf_tensor("stats", [1, 1536], F32))
        fence_sb = En(nc.sbuf_tensor("fence_sb", [1, 8], F32))
        rt = En(nc.sbuf_tensor("rt", [128, 352], F32))
        cw_sb = En(nc.sbuf_tensor("cw_sb", [128, 32], F32))
        ybst = En(nc.sbuf_tensor("ybst", [64, 1024], BF16))
        stage = En(nc.sbuf_tensor("stage", [128, 2 * C], F32))
        x_sb = En(nc.sbuf_tensor("x_sb", [128, NCC * QG + 384], F32))
        xn_sb = En(nc.sbuf_tensor("xn_sb", [128, NCC * QG], BF16))
        h_sb = En(nc.sbuf_tensor("h_sb", [128, 3 * 4 * QG + 384], BF16))
        w_sb = En(nc.sbuf_tensor("w_sb", [128, NSLOT * WCOLS], BF16))
        sil_sb = En(nc.sbuf_tensor("sil_sb", [128, 2 * QG], F32))
        sq_sb = En(nc.sbuf_tensor("sq_sb", [128, 4 * QG], BF16))
        bc_sb = En(nc.sbuf_tensor("bc_sb", [128, 2 * QG], F32))
        htm_sb = En(nc.sbuf_tensor("htm_sb", [128, 2 * QG], BF16))
        exp_sb = En(nc.sbuf_tensor("exp_sb", [128, 2 * 1024], BF16))
        yT_sb = En(nc.sbuf_tensor("yT_sb", [128, NCC * QG], BF16))
        ltT_sb = En(nc.sbuf_tensor("ltT_sb", [128, 2 * QG], BF16))
        ps_a = En(nc.psum_tensor([128, 1024], F32))
        ps_b = En(nc.psum_tensor([128, 1024], F32))
        ps_c = En(nc.psum_tensor([128, 512], F32))
        ps_d = En(nc.psum_tensor([128, 512], F32))
        ps_e = En(nc.psum_tensor([128, 512], F32))
        ps_t = En(nc.psum_tensor([128, 1024], BF16))

        for ci, cv in enumerate(CONSTS):
            nc.const_aps.register(F32, cv, c_sb[:, ci:ci + 1])
        C05 = c_sb[:, CONSTS.index(0.5):CONSTS.index(0.5) + 1]

        sd = Sem(En(nc.semaphore("sd")), "sync", dma=True)
        gd = Sem(En(nc.semaphore("gd")), "gpsimd", dma=True)
        pe = Sem(En(nc.semaphore("pe")), "tensor")
        ac = Sem(En(nc.semaphore("ac")), "scalar")
        ve = Sem(En(nc.semaphore("ve")), "vector")
        blk = En(nc.Block())

        SELFSEM = {"scalar": ac, "vector": ve}

        def E(eng, op, waits=(), inc=None, k=1):
            explicit = inc is not None
            if DBG_SELF_SYNC and eng in SELFSEM:
                sem = SELFSEM[eng]
                waits = list(waits) + [(sem, sem.n, True)]
                if inc is None:
                    inc = sem
            if DBG_DRAIN and explicit and eng in ("tensor", "scalar", "vector"):
                prog.emit(eng, op, waits, None)
                tgt = inc.inc(k)

                def dop(e):
                    return e.drain(fusable=True)
                prog.emit(eng, dop, (), (inc, k))
                return tgt
            tgt = None
            if inc is not None:
                tgt = inc.inc(k)
            prog.emit(eng, op, waits, (inc, k) if inc is not None else None)
            return tgt

        def dma(eng_sem, dst, src, waits=()):
            eng = "sync" if eng_sem is sd else "gpsimd"
            if DBG_DMA_CHAIN and eng_sem.n > 0:
                waits = list(waits) + [(eng_sem, eng_sem.n)]

            def op(e, dst=dst, src=src):
                return e.dma_start(dst, src)
            return E(eng, op, waits, eng_sem, 16)

        def pe_fence():
            def op(t):
                return t.transpose(ps_t[0:1, 768:769], onec[0:1, 0:1], idb[0:1, 0:1])
            return E("tensor", op, (), pe)

        def mm(out, lhsT, rhs, start, stop, tp=None, waits=(), inc=None):
            def op(t, out=out, lhsT=lhsT, rhs=rhs, start=start, stop=stop, tp=tp):
                return t.matmul(out, lhsT, rhs, start=start, stop=stop, tile_position=tp)
            if (DBG_PE_FENCE or DBG_XFENCE) and not DBG_DRAIN and inc is pe:
                E("tensor", op, waits, None)
                return pe_fence()
            return E("tensor", op, waits, inc)

        def trans(out, in_, ident, waits=(), inc=None):
            def op(t, out=out, in_=in_, ident=ident):
                return t.transpose(out, in_, ident)
            if (DBG_PE_FENCE or DBG_XFENCE) and not DBG_DRAIN and inc is pe:
                E("tensor", op, waits, None)
                return pe_fence()
            return E("tensor", op, waits, inc)

        def act(out, in_, func, bias=0.0, scale=1.0, waits=(), inc=None):
            def op(s, out=out, in_=in_, func=func, bias=bias, scale=scale):
                return s.activation(out, in_, func, bias=bias, scale=scale)
            if DBG_XFENCE and not DBG_DRAIN and inc is not None:
                E("scalar", op, waits, None)

                def fop(s, func=func):
                    return s.activation(fence_sb[0:1, 2:3], c_sb[0:1, 0:1], func)
                return E("scalar", fop, (), inc)
            return E("scalar", op, waits, inc)

        def vop(name, *args, waits=(), inc=None):
            def op(v, name=name, args=args):
                return getattr(v, name)(*args)
            if DBG_XFENCE and not DBG_DRAIN and inc is not None:
                E("vector", op, waits, None)

                def fop(v):
                    return v.tensor_copy(fence_sb[0:1, 0:1], fence_sb[0:1, 1:2])
                return E("vector", fop, (), inc)
            return E("vector", op, waits, inc)

        bank = {}

        def brd(key, tgt):
            bank.setdefault(key, {})[id(tgt[0])] = tgt

        def bwt(key):
            return list(bank.get(key, {}).values())

        dma(sd, idf[:], idf_in)
        dma(sd, idb[:], idb_in)
        dma(sd, onec[:], ones_c)
        dma(sd, oner[:], ones_r)
        dma(sd, mask_sb[:], masks_in)
        t_c = dma(sd, c_sb[:], consts_in)
        t_bwt = dma(gd, basis_wT[:].rearrange("p (cc r) -> p cc r", cc=NCC),
                    basis_wt.rearrange("(cc p) r -> p cc r", p=128))
        t_gwt = dma(gd, gate_wT[:].rearrange("p (cc e) -> p cc e", cc=NCC),
                    gate_wt.rearrange("(cc p) e -> p cc e", p=128))
        t_corr = dma(sd, corr32[:], corr_bc_in)
        t_vbb = dma(sd, vb_bc[:, 0:C], vb_bc_in)
        t_ob = dma(sd, ob_col[:], ob_pc_in)
        t_hr = dma(sd, x_sb[0:64, 0:NH * R], head_res.rearrange("h r s -> r h s"))
        t_core = dma(sd, x_sb[0:64, 1024:1024 + R], core_w)
        vop("memset", fence_sb[0:1, 0:8], 1.0)
        for uc in range(NUC if not DBG_NO_MEMSET else 0):
            vop("memset",
                v_sb[:, uc * VSTRIDE:(uc + 1) * VSTRIDE]
                .rearrange("p (h d) -> p h d", h=NH)[:, :, 64:65], 1.0)

        corr_done = (sd, t_corr)
        vb_done = (sd, t_vbb)
        ob_done = (sd, t_ob)
        tpc = trans(ps_e[0:64, 0:64], x_sb[0:64, 1024:1024 + R], idf[0:64, 0:64],
                    waits=[(sd, t_core)], inc=pe)
        ce = vop("tensor_copy", x_sb[0:64, 1088:1088 + R], ps_e[0:64, 0:64],
                 waits=[(pe, tpc)], inc=ve)
        brd("e", (ve, ce))
        vop("tensor_add", x_sb[0:64, 1152:1216], x_sb[0:64, 1024:1088], x_sb[0:64, 1088:1152])
        vop("tensor_scalar", x_sb[0:64, 1152:1216], x_sb[0:64, 1152:1216], 0.03125, None, ALU.mult,
            waits=[(sd, t_c)])
        vop("tensor_add", x_sb[0:64, 1280:1792], x_sb[0:64, 0:512], x_sb[0:64, 512:1024],
            waits=[(sd, t_hr)])
        vop("tensor_add", x_sb[0:64, 1792:2048], x_sb[0:64, 1280:1536], x_sb[0:64, 1536:1792])
        vop("tensor_add", x_sb[0:64, 2048:2176], x_sb[0:64, 1792:1920], x_sb[0:64, 1920:2048])
        vop("tensor_add", x_sb[0:64, 1216:1280], x_sb[0:64, 2048:2112], x_sb[0:64, 2112:2176])
        vop("tensor_scalar", x_sb[0:64, 1216:1280], x_sb[0:64, 1216:1280], 1.0 / NH, None, ALU.mult)
        vop("tensor_sub", x_sb[0:64, 1216:1280], x_sb[0:64, 1216:1280], x_sb[0:64, 1152:1216])
        for h in range(NH):
            vop("tensor_sub", x_sb[0:64, 2176 + h * R:2176 + (h + 1) * R],
                x_sb[0:64, h * R:(h + 1) * R], x_sb[0:64, 1216:1280])
        hm_lo = vop("tensor_copy", hm2[0:64, :], x_sb[0:64, 2176:2176 + NH * R], inc=ve)
        hm_hi = dma(sd, hm2[64:128, :], hm2[0:64, :], waits=[(ve, hm_lo)])
        hm_done = (sd, hm_hi)

        vwT_last = None
        for cc in range(NCC):
            vwT_last = dma(gd, wT_sb[:, cc * C:(cc + 1) * C],
                           v_wt[cc * 128:(cc + 1) * 128, :])
        vwT_ev = (gd, vwT_last)

        morder = []
        for grp in EGROUPS:
            for e in grp:
                morder.append(("g", e))
                morder.append(("u", e))
            for e in grp:
                morder.append(("d", e))
        midx = {m: i for i, m in enumerate(morder)}

        def wsrc(kind, e):
            if kind == "g":
                ap = Wg[e] if e < NE else Wg_s
                return ap.rearrange("(k p) i -> p k i", p=128)
            if kind == "u":
                ap = Wu[e] if e < NE else Wu_s
                return ap.rearrange("(k p) i -> p k i", p=128)
            ap = Wdm[e] if e < NE else Wd_s
            return ap.rearrange("(k p) c -> p k c", p=128)

        slot_free = [None] * NSLOT

        def xck(cc):
            return x_sb[:, cc * QG:(cc + 1) * QG]

        def xnck(cc):
            return xn_sb[:, cc * QG:(cc + 1) * QG]

        def emit_ln(x_ready):
            sq_free = [None] * 4
            mt = None
            x_reads = []
            for cc in range(NCC):
                par = cc % 2
                cb = vop("tensor_copy", sq_sb[:, par * QG:(par + 1) * QG], xck(cc),
                         waits=[x_ready, sq_free[par]], inc=ve)
                sqt = act(sq_sb[:, (2 + par) * QG:(3 + par) * QG], xck(cc), AF.Square,
                          waits=[x_ready, sq_free[2 + par]], inc=ac)
                x_reads = [(ve, cb), (ac, sqt)]
                w0 = [(ve, cb)] + (bwt("e") if cc == 0 else [])
                mm(ps_e[0:1, 0:QG], onec[:], sq_sb[:, par * QG:(par + 1) * QG],
                   cc == 0, cc == NCC - 1, waits=w0)
                w1 = [(ac, sqt)] + (bwt("c") if cc == 0 else [])
                mt = mm(ps_c[0:1, 0:QG], onec[:], sq_sb[:, (2 + par) * QG:(3 + par) * QG],
                        cc == 0, cc == NCC - 1, waits=w1, inc=pe)
                sq_free[par] = (pe, mt)
                sq_free[2 + par] = (pe, mt)
            vop("tensor_scalar", stats[0:1, 0:QG], ps_e[0:1, 0:QG], 1.0 / C, None, ALU.mult,
                waits=[(pe, mt)])
            vop("tensor_mul", stats[0:1, 512:1024], stats[0:1, 0:QG], stats[0:1, 0:QG])
            vfin = vop("scalar_tensor_tensor", stats[0:1, 512:1024], ps_c[0:1, 0:QG],
                       1.0 / C, stats[0:1, 512:1024], ALU.mult, ALU.subtract, inc=ve)
            brd("e", (ve, vfin))
            brd("c", (ve, vfin))
            aln = act(stats[0:1, 1024:1536], stats[0:1, 512:1024], AF.Ln, bias=1e-5,
                      waits=[(ve, vfin)], inc=ac)
            aex = act(stats[0:1, 512:1024], stats[0:1, 1024:1536], AF.Exp, scale=-0.5,
                      waits=[(ac, aln)], inc=ac)
            bmv = mm(ps_c[:, 0:QG], oner[0:1, :], stats[0:1, 0:QG], True, True,
                     waits=[(ve, vfin)] + bwt("c"), inc=pe)
            brv = mm(ps_d[:, 0:QG], oner[0:1, :], stats[0:1, 512:1024], True, True,
                     waits=[(ac, aex)] + bwt("d"), inc=pe)
            c0 = vop("tensor_copy", bc_sb[:, 0:QG], ps_c[:, 0:QG], waits=[(pe, bmv)], inc=ve)
            c1 = vop("tensor_copy", bc_sb[:, QG:2 * QG], ps_d[:, 0:QG], waits=[(pe, brv)], inc=ve)
            brd("c", (ve, c0))
            brd("d", (ve, c1))
            return (ve, c1), x_reads

        xn_state = {"call": 0, "pending": None}

        def emit_xn(bc_t):
            t = None
            for cc in range(NCC):
                par = cc % 2
                vop("tensor_sub", sil_sb[:, par * QG:(par + 1) * QG], xck(cc), bc_sb[:, 0:QG],
                    waits=[bc_t, xn_state["pending"] if cc == 0 else None])
                t = vop("tensor_mul", xnck(cc), sil_sb[:, par * QG:(par + 1) * QG],
                        bc_sb[:, QG:2 * QG], inc=ve)
            return (ve, t)

        def emit_routing(xn_t):
            lmt = None
            for tt in range(4):
                for cc in range(NCC):
                    w = [xn_t] + ([(gd, t_gwt)] + bwt("e") if (tt == 0 and cc == 0) else [])
                    lmt = mm(ps_e[:, tt * 8:(tt + 1) * 8],
                             xn_sb[:, cc * QG + tt * 128:cc * QG + (tt + 1) * 128],
                             gate_wT[:, cc * NE:(cc + 1) * NE],
                             cc == 0, cc == NCC - 1,
                             waits=w, inc=pe if cc == NCC - 1 else None)
            sgt = act(rt[:, 0:32], ps_e[:, 0:32], AF.Sigmoid,
                      waits=[(pe, lmt), route_state["pending"]], inc=ac)
            brd("e", (ac, sgt))

            def rvop(name, *args, waits=(), inc=None):
                if DBG_ROUTE_SYNC:
                    w = list(waits) + [(ve, ve.n, True)]
                    return vop(name, *args, waits=w, inc=ve)
                return vop(name, *args, waits=waits, inc=inc)
            S, SC, G, GR, GS, M8, S4, ER, ES, TW = 0, 32, 64, 80, 96, 112, 144, 176, 208, 240
            DEN, DINV, TMP = 272, 280, 288
            rvop("tensor_add", rt[:, SC:SC + 32], rt[:, S:S + 32], corr32[:],
                waits=[(ac, sgt), corr_done])
            scv = rt[:, SC:SC + 32].rearrange("p (t e) -> p t e", t=4)
            gv = rt[:, G:G + 16].rearrange("p (t g) -> p t g", t=4)
            rvop("tensor_tensor", gv, scv[:, :, 0:8:2], scv[:, :, 1:8:2], ALU.add)
            rvop("memset", rt[:, GR:GR + 16], 0.0)
            grv = rt[:, GR:GR + 16].rearrange("p (t g) -> p t g", t=4)
            tmpv = rt[:, TMP:TMP + 32].rearrange("p (t g) -> p t g", t=4)
            for d in range(1, 4):
                w = 4 - d
                rvop("tensor_tensor", tmpv[:, :, 0:w], gv[:, :, 0:w], gv[:, :, d:4], ALU.is_ge)
                rvop("tensor_tensor", grv[:, :, d:4], grv[:, :, d:4], tmpv[:, :, 0:w], ALU.add)
                rvop("tensor_tensor", tmpv[:, :, 0:w], gv[:, :, d:4], gv[:, :, 0:w], ALU.is_gt)
                rvop("tensor_tensor", grv[:, :, 0:w], grv[:, :, 0:w], tmpv[:, :, 0:w], ALU.add)
            rvop("tensor_scalar", rt[:, GS:GS + 16], rt[:, GR:GR + 16], 2.0, None, ALU.is_lt)
            gsv = rt[:, GS:GS + 16].rearrange("p (t g) -> p t g", t=4)
            m8v = rt[:, M8:M8 + 32].rearrange("p (t e) -> p t e", t=4)
            rvop("tensor_copy", m8v[:, :, 0:8:2], gsv)
            rvop("tensor_copy", m8v[:, :, 1:8:2], gsv)
            rvop("tensor_mul", rt[:, S4:S4 + 32], rt[:, M8:M8 + 32], rt[:, SC:SC + 32])
            rvop("memset", rt[:, ER:ER + 32], 0.0)
            s4v = rt[:, S4:S4 + 32].rearrange("p (t e) -> p t e", t=4)
            erv = rt[:, ER:ER + 32].rearrange("p (t e) -> p t e", t=4)
            tmv = rt[:, TMP:TMP + 64].rearrange("p (t e) -> p t e", t=8)[:, 0:4, :]
            for d in range(1, 8):
                w = 8 - d
                rvop("tensor_tensor", tmv[:, :, 0:w], s4v[:, :, 0:w], s4v[:, :, d:8], ALU.is_ge)
                rvop("tensor_tensor", erv[:, :, d:8], erv[:, :, d:8], tmv[:, :, 0:w], ALU.add)
                rvop("tensor_tensor", tmv[:, :, 0:w], s4v[:, :, d:8], s4v[:, :, 0:w], ALU.is_gt)
                rvop("tensor_tensor", erv[:, :, 0:w], erv[:, :, 0:w], tmv[:, :, 0:w], ALU.add)
            rvop("tensor_scalar", rt[:, ES:ES + 32], rt[:, ER:ER + 32], 4.0, None, ALU.is_lt)
            rvop("tensor_mul", rt[:, TW:TW + 32], rt[:, ES:ES + 32], rt[:, S:S + 32])
            twv = rt[:, TW:TW + 32].rearrange("p (t e) -> p t e", t=4)
            rvop("tensor_reduce", rt[:, DEN:DEN + 4], twv, AX.X, ALU.add)
            rvop("tensor_scalar", rt[:, DEN:DEN + 4], rt[:, DEN:DEN + 4], 1e-20, None, ALU.add)
            rvop("reciprocal", rt[:, DINV:DINV + 4], rt[:, DEN:DEN + 4])
            rvop("tensor_scalar", rt[:, DINV:DINV + 4], rt[:, DINV:DINV + 4], 1.25, None, ALU.mult)
            cw_t = None
            for tt in range(4):
                cw_t = rvop("tensor_scalar", cw_sb[:, tt * 8:(tt + 1) * 8],
                           rt[:, TW + tt * 8:TW + (tt + 1) * 8],
                           rt[:, DINV + tt:DINV + tt + 1], None, ALU.mult,
                           waits=[route_state["pending"]] if tt == 0 else (),
                           inc=ve)
            return (ve, cw_t)

        ffn_state = {"sf": slot_free, "call": 0}
        route_state = {"call": 0, "pending": None}

        def emit_ffn(xn_t, cw_t, x_writable):
            call = ffn_state["call"]
            ffn_state["call"] += 1
            sf = ffn_state["sf"]
            load_tgt = {}

            def ringload(i):
                s = i % NSLOT
                w = [sf[s]]
                if call == 0:
                    kind, e = morder[i]
                    tl = dma(gd, w_sb[:, s * WCOLS:(s + 1) * WCOLS], wsrc(kind, e),
                             waits=w)
                    dma(gd, wbf[i], w_sb[:, s * WCOLS:(s + 1) * WCOLS],
                        waits=[(gd, tl)])
                    load_tgt[i] = tl
                else:
                    load_tgt[i] = dma(gd, w_sb[:, s * WCOLS:(s + 1) * WCOLS], wbf[i],
                                      waits=w)

            x_upd = []
            gpar_free = [None, None]
            upar_free = [None, None]
            sil_free = [None, None]
            htm_free = [None, None]
            h_free = [None, None, None]
            last_gu = None
            for gi, grp in enumerate(EGROUPS):
                if DBG_GROUPS is not None and gi not in DBG_GROUPS:
                    continue
                h_evac = []
                for ei, e in enumerate(grp):
                    ig, iu = midx[("g", e)], midx[("u", e)]
                    ringload(ig)
                    ringload(iu)
                    sg, su = ig % NSLOT, iu % NSLOT
                    wgs = w_sb[:, sg * WCOLS:(sg + 1) * WCOLS]
                    wus = w_sb[:, su * WCOLS:(su + 1) * WCOLS]
                    for tt in range(4):
                        par = tt % 2
                        pg = ps_a[:, par * 512:(par + 1) * 512]
                        pu = ps_b[:, par * 512:(par + 1) * 512]
                        kg, ku = "a%d" % par, "b%d" % par
                        gmt = umt = None
                        for cc in range(NCC):
                            w = []
                            if cc == 0:
                                w = [(gd, load_tgt[ig]), xn_t, gpar_free[par]] + bwt(kg)
                            gmt = mm(pg, xn_sb[:, cc * QG + tt * 128:cc * QG + (tt + 1) * 128],
                                     wgs[:, cc * 512:(cc + 1) * 512], cc == 0, cc == NCC - 1,
                                     waits=w, inc=pe if cc == NCC - 1 else None)
                        for cc in range(NCC):
                            w = []
                            if cc == 0:
                                w = [(gd, load_tgt[iu]), upar_free[par]] + bwt(ku)
                            umt = mm(pu, xn_sb[:, cc * QG + tt * 128:cc * QG + (tt + 1) * 128],
                                     wus[:, cc * 512:(cc + 1) * 512], cc == 0, cc == NCC - 1,
                                     waits=w, inc=pe if cc == NCC - 1 else None)
                        last_gu = umt
                        silt = act(sil_sb[:, par * QG:(par + 1) * QG], pg, AF.Silu,
                                   waits=[(pe, gmt), sil_free[par]], inc=ac)
                        gpar_free[par] = (ac, silt)
                        brd(kg, (ac, silt))
                        scalar_ap = C05 if e == NE else cw_sb[:, tt * 8 + e:tt * 8 + e + 1]
                        htt = vop("scalar_tensor_tensor", htm_sb[:, par * QG:(par + 1) * QG],
                                  pu, scalar_ap, sil_sb[:, par * QG:(par + 1) * QG],
                                  ALU.mult, ALU.mult,
                                  waits=[(pe, umt), (ac, silt), cw_t, htm_free[par]], inc=ve)
                        upar_free[par] = (ve, htt)
                        brd(ku, (ve, htt))
                        sil_free[par] = (ve, htt)
                        tpt = None
                        for ic in range(4):
                            w = [(ve, htt)] + (bwt("t") if ic == 0 else [])
                            tpt = trans(ps_t[:, ic * 128:(ic + 1) * 128],
                                        htm_sb[:, par * QG + ic * 128:par * QG + (ic + 1) * 128],
                                        idb[:], waits=w, inc=pe if ic == 3 else None)
                        htm_free[par] = (pe, tpt)
                        base = ei * 4 * QG + tt * 128
                        dstv = h_sb[:, base:base + 2048].rearrange("p (ic t) -> p ic t", ic=4)[:, :, 0:128]
                        hev = vop("tensor_copy", dstv,
                                  ps_t[:, 0:512].rearrange("p (ic t) -> p ic t", ic=4),
                                  waits=[(pe, tpt), h_free[ei] if tt == 0 else None], inc=ve)
                        brd("t", (ve, hev))
                        h_evac.append((ve, hev))
                    sf[ig % NSLOT] = (pe, gmt)
                    sf[iu % NSLOT] = (pe, umt)
                for e in grp:
                    ringload(midx[("d", e)])
                dmt_last = None
                for cc in range(NCC):
                    par = cc % 2
                    pd_ = ps_c if par == 0 else ps_d
                    kk = "c" if par == 0 else "d"
                    first = True
                    dmt = None
                    for ei, e in enumerate(grp):
                        idd = midx[("d", e)]
                        sdt = idd % NSLOT
                        wds = w_sb[:, sdt * WCOLS:(sdt + 1) * WCOLS]
                        for ic in range(4):
                            w = [(gd, load_tgt[idd])]
                            if first:
                                w = list(h_evac) + w + bwt(kk)
                            dmt = mm(pd_[:, 0:512],
                                     wds[:, ic * C + cc * 128:ic * C + (cc + 1) * 128],
                                     h_sb[:, ei * 4 * QG + ic * QG:ei * 4 * QG + (ic + 1) * QG],
                                     first, (ei == len(grp) - 1 and ic == 3),
                                     waits=w, inc=pe if (ei == len(grp) - 1 and ic == 3) else None)
                            first = False
                    xu = vop("tensor_tensor", xck(cc), xck(cc), pd_[:, 0:512], ALU.add,
                             waits=[(pe, dmt)] + list(x_writable), inc=ve)
                    brd(kk, (ve, xu))
                    x_upd.append((ve, xu))
                    dmt_last = dmt
                for ei in range(len(grp)):
                    h_free[ei] = (pe, dmt_last)
                for e in grp:
                    sf[midx[("d", e)] % NSLOT] = (ve, x_upd[-1][1])
            return x_upd[-NCC:], [(pe, last_gu)]

        stage_free = [vwT_ev, vwT_ev]
        x_free = [hm_done, corr_done, ob_done, vb_done, vwT_ev, (pe, tpc)]
        xn_free_g = []
        lat_last = None
        v_last = None
        bounce_tgt = {}

        for qg in range(DBG_NQG):
            x_evac_last = None
            for tt in range(4):
                sl = tt % 2
                tld = dma(sd, stage[:, sl * C:(sl + 1) * C],
                          x_in[qg * QG + tt * 128:qg * QG + (tt + 1) * 128, :],
                          waits=[stage_free[sl]])
                for half in range(2):
                    pd_ = ps_c if half == 0 else ps_d
                    kk = "c" if half == 0 else "d"
                    tpt = None
                    for j in range(4):
                        cc = half * 4 + j
                        w2 = [(sd, tld)]
                        if j == 0:
                            w2 += bwt(kk)
                        if tt == 0 and cc == 0:
                            w2 += x_free
                        tpt = trans(pd_[:, j * 128:(j + 1) * 128],
                                    stage[:, sl * C + cc * 128:sl * C + (cc + 1) * 128],
                                    idf[:], waits=w2, inc=pe if j == 3 else None)
                    base = half * 4 * QG + tt * 128
                    dstv = x_sb[:, base:base + 2048].rearrange("p (c t) -> p c t", c=4)[:, :, 0:128]
                    ev = vop("tensor_copy", dstv,
                             pd_[:, 0:512].rearrange("p (c t) -> p c t", c=4),
                             waits=[(pe, tpt)], inc=ve)
                    brd(kk, (ve, ev))
                    x_evac_last = (ve, ev)
                stage_free[sl] = (pe, tpt)

            if STAGE >= 2:
                bc_t, x_reads = emit_ln(x_evac_last)
                xn_t = emit_xn(bc_t)
                if DBG_NO_ROUTE:
                    zcw = vop("memset", cw_sb[:], 0.0, inc=ve)
                    cw_t = (ve, zcw)
                else:
                    cw_t = emit_routing(xn_t)
                if DBG_NO_FFN:
                    x_upd = [x_evac_last, cw_t, xn_t]
                else:
                    x_upd, xn_free_g = emit_ffn(xn_t, cw_t, [x_evac_last] + xn_free_g)
            else:
                x_upd, x_reads = [x_evac_last], []

            bt = dma(sd, x1s.rearrange("p (c t) -> p c t", c=NCC)[:, :, qg * QG:(qg + 1) * QG],
                     x_sb[:, 0:NCC * QG].rearrange("p (c t) -> p c t", c=NCC),
                     waits=list(x_upd))
            bounce_tgt[qg] = bt
            x_free = [(sd, bt)] + x_reads

            if STAGE >= 3:
                bc_t2, x_reads2 = emit_ln(x_upd[-1])
                xn_t2 = emit_xn(bc_t2)
                x_free = x_free + x_reads2
                lmt2 = None
                for cc in range(NCC):
                    w = [xn_t2] + ([(gd, t_bwt)] + bwt("c") if cc == 0 else [])
                    lmt2 = mm(ps_c[0:64, 0:512], basis_wT[:, cc * R:(cc + 1) * R],
                              xnck(cc), cc == 0, cc == NCC - 1, waits=w,
                              inc=pe if cc == NCC - 1 else None)
                lc = vop("tensor_copy", latT2[0:64, qg * QG:(qg + 1) * QG], ps_c[0:64, 0:512],
                         waits=[(pe, lmt2)], inc=ve)
                brd("c", (ve, lc))
                ld2 = dma(sd, latT2[64:128, qg * QG:(qg + 1) * QG],
                          latT2[0:64, qg * QG:(qg + 1) * QG], waits=[(ve, lc)])
                lat_last = (sd, ld2)
                vmt = None
                vev = None
                for tt in range(4):
                    for dh in range(2):
                        pv = ps_d if dh == 0 else ps_e
                        kk = "d" if dh == 0 else "e"
                        for cc in range(NCC):
                            w = [xn_t2] + (bwt(kk) if cc == 0 else [])
                            vmt = mm(pv[:, 0:512],
                                     xn_sb[:, cc * QG + tt * 128:cc * QG + (tt + 1) * 128],
                                     wT_sb[:, cc * C + dh * 512:cc * C + (dh + 1) * 512],
                                     cc == 0, cc == NCC - 1, waits=w,
                                     inc=pe if cc == NCC - 1 else None)
                        uc = qg * 4 + tt
                        dst = v_sb[:, uc * VSTRIDE + dh * 8 * 65:uc * VSTRIDE + (dh + 1) * 8 * 65] \
                            .rearrange("p (h d) -> p h d", h=8)[:, :, 0:64]
                        src = pv[:, 0:512].rearrange("p (h d) -> p h d", h=8)
                        vbv = vb_bc[:, dh * 512:(dh + 1) * 512].rearrange("p (h d) -> p h d", h=8)
                        vev = vop("tensor_tensor", dst, src, vbv, ALU.add,
                                  waits=[(pe, vmt), vb_done], inc=ve)
                        brd(kk, (ve, vev))
                v_last = (ve, vev)
                xn_free_g = [(pe, vmt)]

        if STAGE >= 3:
            owT_last = None
            for cc in range(NCC):
                owT_last = dma(gd, wT_sb[:, cc * C:(cc + 1) * C],
                               o_wt[cc * 128:(cc + 1) * 128, :],
                               waits=(xn_free_g + [v_last]) if cc == 0 else [])
            owt_ready = (gd, owT_last)
        else:
            owt_ready = None

        exp_free = [None, None]
        lt_free = [None, None]
        ybst_free = [None, None]
        for qg in range(DBG_NQG):
            tld = dma(sd, x_sb[:, 0:NCC * QG].rearrange("p (c t) -> p c t", c=NCC),
                      x1s.rearrange("p (c t) -> p c t", c=NCC)[:, :, qg * QG:(qg + 1) * QG],
                      waits=x_free + [(sd, bounce_tgt[qg])])
            x_ready = (sd, tld)
            x_upd2 = [x_ready]

            if STAGE >= 3:
                nuc = 4 * (qg + 1)
                lt_dma = {}

                def emit_ltT(p):
                    lpar = p % 2
                    hA, hB = 2 * p, 2 * p + 1
                    w = [lat_last, hm_done] + bwt("e")
                    la = mm(ps_e[0:64, 0:512], hm2[0:64, hA * R:(hA + 1) * R],
                            latT2[0:64, qg * QG:(qg + 1) * QG], True, True, tp=(0, 0),
                            waits=w, inc=pe)
                    e0 = vop("tensor_copy", ltT_sb[0:64, lpar * QG:(lpar + 1) * QG],
                             ps_e[0:64, 0:512], waits=[(pe, la), lt_free[lpar]], inc=ve)
                    lb = mm(ps_e[0:64, 0:512], hm2[0:64, hB * R:(hB + 1) * R],
                            latT2[0:64, qg * QG:(qg + 1) * QG], True, True, tp=(0, 0),
                            waits=[(ve, e0)], inc=pe)
                    e1 = vop("tensor_copy", ybst[:, lpar * 512:(lpar + 1) * 512],
                             ps_e[0:64, 0:512], waits=[(pe, lb), ybst_free[lpar]], inc=ve)
                    brd("e", (ve, e1))
                    dmt = dma(sd, ltT_sb[64:128, lpar * QG:(lpar + 1) * QG],
                              ybst[:, lpar * 512:(lpar + 1) * 512], waits=[(ve, e1)])
                    ybst_free[lpar] = (sd, dmt)
                    lt_dma[p] = [(ve, e0), (sd, dmt)]

                emit_ltT(0)
                for hp in range(NH // 2):
                    lpar = hp % 2
                    hA, hB = 2 * hp, 2 * hp + 1
                    lt_ready = lt_dma[hp]
                    avB = None
                    for j in range(nuc):
                        par = j % 2
                        psc = ps_a if par == 0 else ps_b
                        k0, k1 = ("a0", "a1") if par == 0 else ("b0", "b1")
                        sa = mm(psc[:, 0:512], latT2[0:64, j * 128:(j + 1) * 128],
                                ltT_sb[0:64, lpar * QG:(lpar + 1) * QG], True, True,
                                tp=(0, 0), waits=lt_ready + [v_last] + bwt(k0))
                        sb2 = mm(psc[:, 512:1024], latT2[64:128, j * 128:(j + 1) * 128],
                                 ltT_sb[64:128, lpar * QG:(lpar + 1) * QG], True, True,
                                 tp=(64, 0), waits=bwt(k1), inc=pe)
                        ext = act(exp_sb[:, par * 1024:(par + 1) * 1024], psc[:, 0:1024],
                                  AF.Exp, scale=0.125,
                                  waits=[(pe, sb2), exp_free[par]], inc=ac)
                        brd(k0, (ac, ext))
                        brd(k1, (ac, ext))
                        esrc = exp_sb[:, par * 1024:(par + 1) * 1024]
                        delta = j * 128 - qg * QG
                        if delta >= 0:
                            msl = mask_sb[:, 384 - delta:896 - delta]
                            vop("tensor_tensor", esrc[:, 0:512], esrc[:, 0:512],
                                msl, ALU.mult, waits=[(ac, ext)])
                            emk = vop("tensor_tensor", esrc[:, 512:1024], esrc[:, 512:1024],
                                      msl, ALU.mult, inc=ve)
                            eready = (ve, emk)
                        else:
                            eready = (ac, ext)
                        wA = [eready] + (bwt("c") if j == 0 else [])
                        mm(ps_c[0:65, 0:512],
                           v_sb[:, j * VSTRIDE + hA * 65:j * VSTRIDE + (hA + 1) * 65],
                           esrc[:, 0:512], j == 0, j == nuc - 1, waits=wA)
                        wB = bwt("d") if j == 0 else []
                        avB = mm(ps_d[0:65, 0:512],
                                 v_sb[:, j * VSTRIDE + hB * 65:j * VSTRIDE + (hB + 1) * 65],
                                 esrc[:, 512:1024], j == 0, j == nuc - 1, waits=wB, inc=pe)
                        exp_free[par] = (pe, avB)
                        if j == 0 and hp + 1 < NH // 2:
                            emit_ltT(hp + 1)
                    av_done = (pe, avB)
                    lt_free[lpar] = av_done
                    r1 = vop("reciprocal", scr_f32[64:65, 512:1024], ps_c[64:65, 0:512],
                             waits=[av_done], inc=ve)
                    ba = mm(ps_e[:, 0:512], oner[64:65, :], scr_f32[64:65, 512:1024],
                            True, True, tp=(64, 0), waits=[(ve, r1)] + bwt("e"), inc=pe)
                    b0 = vop("tensor_copy", scr_f32[:, 0:512], ps_e[:, 0:512],
                             waits=[(pe, ba)], inc=ve)
                    brd("e", (ve, b0))
                    ya = vop("tensor_tensor", yT_sb[0:64, hp * QG:(hp + 1) * QG],
                             ps_c[0:64, 0:512], scr_f32[0:64, 0:512], ALU.mult,
                             waits=[(ve, b0)] + bwt("yt"), inc=ve)
                    brd("c", (ve, ya))
                    r2 = vop("reciprocal", scr_f32[64:65, 512:1024], ps_d[64:65, 0:512],
                             waits=[(pe, ba), av_done], inc=ve)
                    bb = mm(ps_e[:, 0:512], oner[64:65, :], scr_f32[64:65, 512:1024],
                            True, True, tp=(64, 0), waits=[(ve, r2), (ve, b0)], inc=pe)
                    b1 = vop("tensor_copy", scr_f32[:, 0:512], ps_e[:, 0:512],
                             waits=[(pe, bb), (ve, ya)], inc=ve)
                    brd("e", (ve, b1))
                    yb = vop("tensor_tensor", ybst[:, lpar * 512:(lpar + 1) * 512],
                             ps_d[0:64, 0:512], scr_f32[0:64, 0:512], ALU.mult,
                             waits=[(ve, b1), ybst_free[lpar]], inc=ve)
                    brd("d", (ve, yb))
                    ydm = dma(sd, yT_sb[64:128, hp * QG:(hp + 1) * QG],
                              ybst[:, lpar * 512:(lpar + 1) * 512],
                              waits=[(ve, yb)] + bwt("yt"))
                    ybst_free[lpar] = (sd, ydm)
                    x_upd2.append((sd, ydm))
                yt_dma_done = x_upd2[-1]
                x_upd2 = [x_ready]
                bank.pop("yt", None)
                omt = None
                for cc in range(NCC):
                    par = cc % 2
                    po = ps_c if par == 0 else ps_d
                    kk = "c" if par == 0 else "d"
                    for dc in range(NCC):
                        w = []
                        if dc == 0:
                            w = [yt_dma_done, owt_ready] + bwt(kk)
                        omt = mm(po[:, 0:512], wT_sb[:, dc * C + cc * 128:dc * C + (cc + 1) * 128],
                                 yT_sb[:, dc * QG:(dc + 1) * QG], dc == 0, dc == NCC - 1,
                                 waits=w, inc=pe if dc == NCC - 1 else None)
                    vop("tensor_tensor", xck(cc), xck(cc), po[:, 0:512], ALU.add,
                        waits=[(pe, omt), x_ready, ob_done])
                    xu = vop("tensor_scalar", xck(cc), xck(cc), ob_col[:, cc:cc + 1],
                             None, ALU.add, inc=ve)
                    brd(kk, (ve, xu))
                    x_upd2.append((ve, xu))
                brd("yt", (pe, omt))
                x_upd2 = x_upd2[-NCC:]

            if STAGE >= 4:
                bc_t3, _xr3 = emit_ln(x_upd2[-1])
                xn_t3 = emit_xn(bc_t3)
                if DBG_NO_ROUTE:
                    zcw3 = vop("memset", cw_sb[:], 0.0, inc=ve)
                    cw_t3 = (ve, zcw3)
                else:
                    cw_t3 = emit_routing(xn_t3)
                if not DBG_NO_FFN:
                    x_upd2, xn_free_g = emit_ffn(xn_t3, cw_t3, x_upd2)

            x_free = []
            for tt in range(4):
                sl = tt % 2
                ev_t = None
                for half in range(2):
                    pd_ = ps_c if half == 0 else ps_d
                    kk = "c" if half == 0 else "d"
                    tpt = None
                    for j in range(4):
                        cc = half * 4 + j
                        w2 = []
                        if j == 0:
                            w2 = bwt(kk)
                        if tt == 0 and cc == 0:
                            w2 = w2 + list(x_upd2)
                        tpt = trans(pd_[:, j * 128:(j + 1) * 128],
                                    x_sb[:, cc * QG + tt * 128:cc * QG + (tt + 1) * 128],
                                    idf[:], waits=w2, inc=pe if j == 3 else None)
                    ev_t = vop("tensor_copy",
                               stage[:, sl * C + half * 512:sl * C + (half + 1) * 512],
                               pd_[:, 0:512],
                               waits=[(pe, tpt), stage_free[sl] if half == 0 else None], inc=ve)
                    brd(kk, (ve, ev_t))
                tst = dma(sd, out_hbm[qg * QG + tt * 128:qg * QG + (tt + 1) * 128, :],
                          stage[:, sl * C:(sl + 1) * C], waits=[(ve, ev_t)])
                stage_free[sl] = (sd, tst)
                x_free = [(pe, tpt)]

        prog.emit("sync", lambda e, n=sd.n: e.wait_ge(sd.h, n))
        prog.emit("gpsimd", lambda e, n=gd.n: e.wait_ge(gd.h, n))

        for eng in ENGS:
            fns = prog.th[eng]
            getattr(blk, eng)(lambda e, fns=fns: [f(e) for f in fns])

    return nc


_NC_CACHE = {}


def _get_nc():
    if "nc" not in _NC_CACHE:
        _NC_CACHE["nc"] = build_program()
    return _NC_CACHE["nc"]


def kernel(L0_attn_hidden_states, L0_attn_basis_w, L0_attn_core,
           L0_attn_head_residual, L0_attn_v_w, L0_attn_v_b, L0_attn_o_w,
           L0_attn_o_b, L0_ffn_gate_w, L0_ffn_correction_bias, L0_ffn_Wg,
           L0_ffn_Wu, L0_ffn_Wd, L0_ffn_Wg_s, L0_ffn_Wu_s, L0_ffn_Wd_s):
    x = np.ascontiguousarray(np.asarray(L0_attn_hidden_states, dtype=np.float32))

    def f32(a):
        return np.ascontiguousarray(np.asarray(a, dtype=np.float32))

    eye = np.eye(128, dtype=np.float32)
    pp = np.arange(128)[:, None]
    qq = np.arange(896)[None, :]
    masks = (pp + 384 <= qq).astype(BF)
    consts = np.tile(np.array([CONSTS], np.float32), (128, 1))

    shared = {
        "basis_wt": f32(np.asarray(L0_attn_basis_w, np.float32).T),
        "core_w": f32(L0_attn_core),
        "head_res": f32(L0_attn_head_residual),
        "v_wt": f32(np.asarray(L0_attn_v_w, np.float32).T),
        "o_wt": f32(np.asarray(L0_attn_o_w, np.float32).T),
        "gate_wt": f32(np.asarray(L0_ffn_gate_w, np.float32).T),
        "corr_bc": np.tile(f32(L0_ffn_correction_bias).reshape(1, NE), (128, 4)),
        "vb_bc": np.tile(f32(L0_attn_v_b).reshape(1, C), (128, 1)),
        "ob_pc": f32(np.asarray(L0_attn_o_b, np.float32).reshape(NCC, 128).T),
        "Wg": f32(L0_ffn_Wg),
        "Wu": f32(L0_ffn_Wu),
        "Wdm": f32(L0_ffn_Wd),
        "Wg_s": f32(L0_ffn_Wg_s),
        "Wu_s": f32(L0_ffn_Wu_s),
        "Wd_s": f32(L0_ffn_Wd_s),
        "idf": eye,
        "idb": eye.astype(BF),
        "ones_c": np.ones((128, 1), BF),
        "ones_r": np.ones((65, 128), np.float32),
        "masks": masks,
        "consts": consts,
    }
    in_maps = []
    for b in range(8):
        m = dict(shared)
        m["x_in"] = np.ascontiguousarray(x[b])
        in_maps.append(m)

    nc = _get_nc()
    res = run_bass_kernel_spmd(nc, in_maps, core_ids=list(range(8)))
    out = np.stack([np.asarray(r["out"], dtype=np.float32) for r in res.results], axis=0)
    return out
